# version 02
# baseline (speedup 1.0000x reference)
import contextlib
import numpy as np
import ml_dtypes
import concourse.bass as bass
import concourse.mybir as mybir
from concourse.bass_utils import run_bass_kernel_spmd

F32 = mybir.dt.float32
BF16 = mybir.dt.bfloat16
AF = mybir.ActivationFunctionType
OP = mybir.AluOpType
AX = mybir.AxisListType.X

S = 2048
E = 2048
H = 16
HD = 128
NCH = 16
NK = 16
NHG = 4
HPG = 4
WG = 512
EPS = 1e-5
QSCALE = float(HD) ** -0.5
INV_E = 1.0 / float(E)
INV_HD = 1.0 / float(HD)
BIGNEG = 1.0e9


def build(debug=False, stop_after=None, nch1=NCH, skip=()):
    nc = bass.Bass("TRN2", target_bir_lowering=False)
    dram = {}

    def dt(name, shape, dtype, kind):
        dram[name] = nc.dram_tensor(name, shape, dtype, kind=kind)

    dt("x", [S, E], F32, "ExternalInput")
    for nm in ["Wf", "Wq", "Wk", "Wv", "Wg", "Wo"]:
        dt(nm, [E, E], F32, "ExternalInput")
    dt("lnwT", [128, NK], F32, "ExternalInput")
    dt("lnbT", [128, NK], F32, "ExternalInput")
    for nm in ["bqR", "bkR", "bvR", "bgR", "boR"]:
        dt(nm, [1, E], F32, "ExternalInput")
    dt("gnwB", [128, E], F32, "ExternalInput")
    dt("gnbB", [128, E], F32, "ExternalInput")
    dt("decB", [128, H], F32, "ExternalInput")
    dt("DIFFT", [128, 128], F32, "ExternalInput")
    dt("ROWP1", [128, 1], F32, "ExternalInput")
    dt("R127M", [128, 1], F32, "ExternalInput")
    dt("IDENT", [128, 128], BF16, "ExternalInput")
    dt("y", [S, E], F32, "ExternalOutput")
    sk = "ExternalOutput" if debug else "Internal"
    dt("x1s", [S, E], F32, sk)
    dt("Ps", [S, E], BF16, sk)
    dt("Wts", [128, NK * E], BF16, sk)

    stack = contextlib.ExitStack()

    def sb(name, shape, dtype):
        return stack.enter_context(nc.sbuf_tensor(name, shape, dtype))

    def psum(name, shape, dtype):
        return stack.enter_context(nc.psum_tensor(name, shape, dtype))

    def sem(name):
        return stack.enter_context(nc.semaphore(name))

    xn2T = sb("xn2T", [128, NK * E], BF16)
    big = sb("big", [128, NK * E], BF16)
    znb = [sb(f"znb{i}", [128, E], BF16) for i in range(2)]
    gnwB = sb("gnwBs", [128, E], F32)
    gnbB = sb("gnbBs", [128, E], F32)
    lnwT = sb("lnwTs", [128, NK], F32)
    lnbT = sb("lnbTs", [128, NK], F32)
    decB = sb("decBs", [128, H], F32)
    DIFFT = sb("DIFFTs", [128, 128], F32)
    ROWP1 = sb("ROWP1s", [128, 1], F32)
    R127M = sb("R127Ms", [128, 1], F32)
    IDENT = sb("IDENTs", [128, 128], BF16)
    onesb = sb("onesb", [1, 128], BF16)
    onesf = sb("onesf", [128, 1], F32)
    onesrf = sb("onesrf", [1, 128], F32)
    lnbrowb = sb("lnbrowb", [1, E], BF16)
    halflnw = sb("halflnw", [128, NK], F32)
    lnbTb = sb("lnbTb", [128, NK], BF16)
    erB = sb("erB", [128, H], F32)
    ekB = sb("ekB", [128, H], F32)
    edC = sb("edC", [128, H], F32)
    stt = [sb(f"stt{i}", [128, 48], F32) for i in range(2)]
    wsc = sb("wsc", [128, 1], F32)
    scl1 = sb("scl1", [1, 1], F32)
    epsb = sb("epsb", [128, 1], F32)
    qscb = sb("qscb", [128, 1], F32)
    c128b = sb("c128b", [128, 1], F32)
    psA = psum("psA", [128, 512], F32)
    psB = psum("psB", [128, 512], F32)
    psC = psum("psC", [128, 512], F32)
    psS = psum("psS", [128, 512], F32)
    psL = psum("psL", [128, 512], F32)
    psX = psum("psX", [128, 512], F32)
    psT0 = psum("psT0", [128, 1024], BF16)
    psT1 = psum("psT1", [128, 1024], BF16)
    psTx = [psT0, psT1]
    psU = psC

    xn2T_r = xn2T.rearrange("p (k e) -> p k e", k=NK)
    big_r = big.rearrange("p (k e) -> p k e", k=NK)
    psT_r = [t.rearrange("p (k e) -> p k e", k=8) for t in psTx]

    def nop2(dv):
        dv.drain()

    def stats3(dv, b, base):
        nop2(dv)
        dv.tensor_scalar(stt[b][:, base + 8:base + 9], stt[b][:, base + 0:base + 1], INV_E, None, OP.mult)
        nop2(dv)
        dv.tensor_tensor(stt[b][:, base + 9:base + 10], stt[b][:, base + 8:base + 9],
                         stt[b][:, base + 8:base + 9], OP.mult)
        nop2(dv)
        return dv.scalar_tensor_tensor(stt[b][:, base + 10:base + 11], stt[b][:, base + 4:base + 5], INV_E,
                                       stt[b][:, base + 9:base + 10], OP.mult, OP.subtract)

    def statsfin(dv, b, base):
        dv.reciprocal(stt[b][:, base + 13:base + 14], stt[b][:, base + 12:base + 13])
        nop2(dv)
        return dv.scalar_tensor_tensor(stt[b][:, base + 14:base + 15], stt[b][:, base + 8:base + 9], -1.0,
                                       stt[b][:, base + 13:base + 14], OP.mult, OP.mult)

    stack1 = contextlib.ExitStack()

    def sb1(name, shape, dtype):
        return stack1.enter_context(nc.sbuf_tensor(name, shape, dtype))

    xb = [sb1(f"xb{i}", [128, E], F32) for i in range(2)]
    x1b = [sb1(f"x1b{i}", [128, E], F32) for i in range(2)]
    ub = sb1("ub", [128, E], F32)
    tbuf = sb1("tbuf", [128, E], BF16)
    tbuf_r = tbuf.rearrange("p (k e) -> p k e", k=NK)

    s0_ld = sem("s0ld")
    s0_ac = sem("s0ac")
    s0_dv = sem("s0dv")
    s0_pe = sem("s0pe")

    with nc.Block() as B:
        @B.sync
        def _(sp):
            small = [("lnwT", lnwT), ("lnbT", lnbT), ("gnwB", gnwB), ("gnbB", gnbB),
                     ("decB", decB), ("DIFFT", DIFFT), ("ROWP1", ROWP1),
                     ("R127M", R127M), ("IDENT", IDENT)]
            for nm, buf in small:
                sp.dma_start(buf[:, :], dram[nm].ap()[:, :]).then_inc(s0_ld, 16)
            for k in range(NK):
                if k >= 4:
                    sp.wait_ge(s0_ac, k - 3)
                buf = [xb[0], xb[1], x1b[0], x1b[1]][k % 4]
                sp.dma_start(buf[:, :], dram["Wf"].ap()[128 * k:128 * (k + 1), :]).then_inc(s0_ld, 16)

        @B.scalar
        def _(ac):
            for k in range(NK):
                ac.wait_ge(s0_ld, 16 * (9 + k + 1))
                buf = [xb[0], xb[1], x1b[0], x1b[1]][k % 4]
                ac.activation(ub[:, :], buf[:, :], AF.Abs,
                              accum_out=stt[0][:, k:k + 1])
                ac.drain(); ac.sem_inc(s0_ac, 1)

        @B.vector
        def _(dv):
            dv.memset(onesb[:, :], 1.0)
            dv.memset(onesf[:, :], 1.0)
            dv.memset(onesrf[:, :], 1.0)
            dv.memset(epsb[:, :], EPS)
            dv.memset(qscb[:, :], QSCALE)
            dv.memset(c128b[:, :], 128.0)
            dv.wait_ge(s0_ac, NK)
            dv.reduce_sum(wsc[:, 0:1], stt[0][:, 0:NK], axis=AX)
            dv.drain(); dv.sem_inc(s0_dv, 1)

        @B.tensor
        def _(pe):
            pe.wait_ge(s0_dv, 1)
            pe.matmul(psA[0:1, 0:1], wsc[:, 0:1], onesf[:, 0:1],
                      start=True, stop=True).then_inc(s0_pe, 1)

        @B.gpsimd
        def _(pl):
            pass

    with nc.Block() as B:
        @B.vector
        def _(dv):
            dv.tensor_scalar(scl1[0:1, 0:1], psA[0:1, 0:1], 0.5 * INV_E * INV_E, None, OP.mult)

    s0_bc = sem("s0bc")
    with nc.Block() as B:
        @B.vector
        def _(dv):
            dv.tensor_scalar(onesrf[0:1, 0:128], onesrf[0:1, 0:128], scl1[0:1, 0:1], None, OP.mult)
            dv.tensor_scalar(halflnw[:, :], lnwT[:, :], 0.5, None, OP.mult)
            dv.tensor_copy(lnbTb[:, :], lnbT[:, :])

        @B.sync
        def _(sp):
            pass
    with nc.Block() as B:
        @B.sync
        def _(sp):
            sp.dma_start(dram["x1s"].ap()[0:1, 0:128], onesrf[0:1, 0:128]).then_inc(s0_bc, 16)
            sp.wait_ge(s0_bc, 16)
            src = dram["x1s"].ap()[0:1, 0:128].rearrange("a b -> b a")
            sp.dma_start(wsc[:, 0:1], src).then_inc(s0_bc, 16)
            sp.wait_ge(s0_bc, 32)
        @B.vector
        def _(dv):
            dv.wait_ge(s0_bc, 32)
            dv.memset(onesrf[:, :], 1.0)

        @B.scalar
        def _(ac):
            for h in range(H):
                ac.activation(erB[:, h:h + 1], ROWP1[:, 0:1], AF.Exp, scale=decB[:, h:h + 1])
                ac.activation(ekB[:, h:h + 1], R127M[:, 0:1], AF.Exp, scale=decB[:, h:h + 1])
            ac.activation(edC[:, :], decB[:, :], AF.Exp, scale=c128b[:, 0:1])

    s2_ld = sem("s2ld")
    s2_ac = sem("s2ac")
    s2_dv = sem("s2dv")
    s2_pe = sem("s2pe")
    with nc.Block() as B:
        @B.sync
        def _(sp):
            for k in range(NK):
                if k >= 2:
                    sp.wait_ge(s2_ac, 2 * (k - 1))
                sp.dma_start(xb[k % 2][:, :], dram["Wf"].ap()[128 * k:128 * (k + 1), :]).then_inc(s2_ld, 16)

        @B.scalar
        def _(ac):
            for k in range(NK):
                ac.wait_ge(s2_ld, 16 * (k + 1))
                if k >= 2:
                    ac.wait_ge(s2_pe, k - 1)
                    ac.wait_ge(s2_dv, 2 * (k - 1))
                ac.activation(znb[k % 2][:, :], xb[k % 2][:, :], AF.Sign).then_inc(s2_ac, 1)
                ac.activation(x1b[k % 2][:, :], xb[k % 2][:, :], AF.Abs).then_inc(s2_ac, 1)

        @B.vector
        def _(dv):
            for k in range(NK):
                dv.wait_ge(s2_ac, 2 * (k + 1))
                dv.scalar_tensor_tensor(znb[k % 2][:, :], x1b[k % 2][:, :], wsc[:, 0:1],
                                        znb[k % 2][:, :], OP.is_ge, OP.mult).then_inc(s2_dv, 1)
                dv.tensor_scalar(big_r[:, k, :], znb[k % 2][:, :],
                                 halflnw[:, k:k + 1], None, OP.mult).then_inc(s2_dv, 1)

        @B.tensor
        def _(pe):
            pbank = [psA, psB, psC, psS]
            for k in range(NK):
                pe.wait_ge(s2_dv, 2 * k + 1)
                last = None
                for nb in range(4):
                    last = pe.matmul(pbank[nb][0:1, 0:512], lnbTb[:, k:k + 1],
                                     znb[k % 2][:, 512 * nb:512 * (nb + 1)],
                                     start=(k == 0), stop=(k == NK - 1))
                last.then_inc(s2_pe, 1)

    with nc.Block() as B:
        @B.vector
        def _(dv):
            pbank = [psA, psB, psC, psS]
            for nb in range(4):
                dv.tensor_scalar(lnbrowb[0:1, 512 * nb:512 * (nb + 1)], pbank[nb][0:1, 0:512],
                                 0.5, None, OP.mult)

    if stop_after == "P0":
        dbg_sem = sem("dbgsem")
        with nc.Block() as B:
            @B.sync
            def _(sp):
                sp.dma_start(dram["Wts"].ap()[:, :], big[:, :]).then_inc(dbg_sem, 16)
                with nc.allow_non_contiguous_dma(reason="debug"):
                    sp.dma_start(dram["x1s"].ap()[0:128, 0:1], wsc[:, 0:1]).then_inc(dbg_sem, 16)
                    sp.dma_start(dram["x1s"].ap()[0:128, 8:32], stt[0][:, :]).then_inc(dbg_sem, 16)
                    sp.dma_start(dram["x1s"].ap()[0:1, 40:41], scl1[0:1, 0:1]).then_inc(dbg_sem, 16)
                    sp.dma_start(dram["x1s"].ap()[0:128, 50:51], onesf[:, 0:1]).then_inc(dbg_sem, 16)
                sp.wait_ge(dbg_sem, 80)
        stack1.close()
        return nc, stack

    p1_ld = sem("p1ld")
    p1_pl = sem("p1pl")
    p1_ac = sem("p1ac")
    p1_dv = sem("p1dv")
    p1_pe = sem("p1pe")
    A1, V1, P1 = 9, 21, 12

    with nc.Block() as B:
        @B.sync
        def _(sp):
            for c in range(nch1):
                if c >= 2:
                    sp.wait_ge(p1_dv, V1 * (c - 2) + 14)
                sp.dma_start(xb[c % 2][:, :], dram["x"].ap()[128 * c:128 * (c + 1), :]).then_inc(p1_ld, 16)

        @B.gpsimd
        def _(pl):
            if "wts" in skip:
                pl.sem_inc(p1_pl, 16)
            else:
                pl.dma_start(dram["Wts"].ap()[:, :], big[:, :]).then_inc(p1_pl, 16)
            for c in range(nch1):
                pl.wait_ge(p1_dv, V1 * c + 14)
                if "stores" in skip:
                    pl.sem_inc(p1_pl, 16)
                else:
                    pl.dma_start(dram["x1s"].ap()[128 * c:128 * (c + 1), :], x1b[c % 2][:, :]).then_inc(p1_pl, 16)

        @B.scalar
        def _(ac):
            for c in range(nch1):
                b = c % 2
                ac.wait_ge(p1_ld, 16 * (c + 1))
                if c >= 2:
                    ac.wait_ge(p1_pl, 16 * c)
                    ac.wait_ge(p1_dv, V1 * (c - 2) + V1)
                ac.activation(x1b[b][:, :], xb[b][:, :], AF.Square,
                              accum_out=stt[b][:, 4:5])
                ac.drain(); ac.sem_inc(p1_ac, 1)
                ac.wait_ge(p1_dv, V1 * c + 2)
                if "sqrt" in skip:
                    ac.activation(stt[b][:, 12:13], stt[b][:, 10:11], AF.Sqrt).then_inc(p1_ac, 1)
                else:
                    ac.activation(stt[b][:, 12:13], stt[b][:, 10:11], AF.Sqrt, bias=epsb[:, 0:1])
                ac.drain(); ac.sem_inc(p1_ac, 1)
                ac.wait_ge(p1_dv, V1 * c + 3)
                ac.activation(ub[:, :], xb[b][:, :], AF.Identity, scale=stt[b][:, 13:14],
                              bias=stt[b][:, 14:15]).then_inc(p1_ac, 1)
                ac.activation(x1b[b][:, :], ub[:, :], AF.Square,
                              accum_out=stt[b][:, 20:21])
                ac.drain(); ac.sem_inc(p1_ac, 1)
                ac.wait_ge(p1_dv, V1 * c + 5)
                ac.activation(stt[b][:, 28:29], stt[b][:, 26:27], AF.Sqrt, bias=epsb[:, 0:1])
                ac.drain(); ac.sem_inc(p1_ac, 1)
                ac.wait_ge(p1_dv, V1 * c + 6)
                ac.activation(znb[b][:, :], ub[:, :], AF.Identity, scale=stt[b][:, 29:30],
                              bias=stt[b][:, 30:31]).then_inc(p1_ac, 1)
                ac.wait_ge(p1_dv, V1 * c + 14)
                ac.activation(ub[:, :], x1b[b][:, :], AF.Square,
                              accum_out=stt[b][:, 36:37])
                ac.drain(); ac.sem_inc(p1_ac, 1)
                ac.wait_ge(p1_dv, V1 * c + 16)
                ac.activation(stt[b][:, 44:45], stt[b][:, 42:43], AF.Sqrt, bias=epsb[:, 0:1])
                ac.drain(); ac.sem_inc(p1_ac, 1)
                ac.wait_ge(p1_dv, V1 * c + 17)
                ac.activation(znb[b][:, :], x1b[b][:, :], AF.Identity, scale=stt[b][:, 45:46],
                              bias=stt[b][:, 46:47]).then_inc(p1_ac, 1)

        @B.vector
        def _(dv):
            for c in range(nch1):
                b = c % 2
                dv.wait_ge(p1_ld, 16 * (c + 1))
                if c >= 2:
                    dv.wait_ge(p1_ac, A1 * (c - 2) + A1)
                dv.reduce_sum(stt[b][:, 0:1], xb[b][:, :], axis=AX)
                dv.drain(); dv.sem_inc(p1_dv, 1)
                dv.wait_ge(p1_ac, A1 * c + 1)
                stats3(dv, b, 0); dv.drain(); dv.sem_inc(p1_dv, 1)
                dv.wait_ge(p1_ac, A1 * c + 2)
                statsfin(dv, b, 0); dv.drain(); dv.sem_inc(p1_dv, 1)
                dv.wait_ge(p1_ac, A1 * c + 3)
                dv.reduce_sum(stt[b][:, 16:17], ub[:, :], axis=AX)
                dv.drain(); dv.sem_inc(p1_dv, 1)
                dv.wait_ge(p1_ac, A1 * c + 4)
                stats3(dv, b, 16); dv.drain(); dv.sem_inc(p1_dv, 1)
                dv.wait_ge(p1_ac, A1 * c + 5)
                statsfin(dv, b, 16); dv.drain(); dv.sem_inc(p1_dv, 1)
                for g in range(4):
                    dv.wait_ge(p1_pe, P1 * c + g + 1)
                    if "t1" not in skip:
                        dv.tensor_copy(tbuf_r[:, 4 * g:4 * g + 4, :],
                                       psT_r[g % 2][:, 0:4, :]).then_inc(p1_dv, 1)
                    else:
                        dv.sem_inc(p1_dv, 1)
                for nb in range(4):
                    dv.wait_ge(p1_pe, P1 * c + 4 + nb + 1)
                    bank = psA if nb % 2 == 0 else psB
                    if "ffn" in skip:
                        dv.tensor_copy(x1b[b][:, 512 * nb:512 * (nb + 1)],
                                       xb[b][:, 512 * nb:512 * (nb + 1)]).then_inc(p1_dv, 1)
                    else:
                        dv.tensor_add(x1b[b][:, 512 * nb:512 * (nb + 1)],
                                      xb[b][:, 512 * nb:512 * (nb + 1)],
                                      bank[:, 0:512]).then_inc(p1_dv, 1)
                dv.reduce_sum(stt[b][:, 32:33], x1b[b][:, :], axis=AX)
                dv.drain(); dv.sem_inc(p1_dv, 1)
                dv.wait_ge(p1_ac, A1 * c + 7)
                stats3(dv, b, 32); dv.drain(); dv.sem_inc(p1_dv, 1)
                dv.wait_ge(p1_ac, A1 * c + 8)
                statsfin(dv, b, 32); dv.drain(); dv.sem_inc(p1_dv, 1)
                for g in range(4):
                    dv.wait_ge(p1_pe, P1 * c + 8 + g + 1)
                    if "t2" not in skip:
                        dv.tensor_copy(xn2T_r[:, 4 * g:4 * g + 4, 128 * c:128 * (c + 1)],
                                       psT_r[g % 2][:, 0:4, :]).then_inc(p1_dv, 1)
                    else:
                        dv.sem_inc(p1_dv, 1)

        @B.tensor
        def _(pe):
            for c in range(nch1):
                b = c % 2
                pe.wait_ge(p1_ac, A1 * c + 6)
                for g in range(4):
                    if g >= 2:
                        pe.wait_ge(p1_dv, V1 * c + 6 + (g - 2) + 1)
                    elif c > 0:
                        pe.wait_ge(p1_dv, V1 * (c - 1) + 17 + (g + 2) + 1)
                    if "t1" not in skip:
                        for t in range(4):
                            k = 4 * g + t
                            last = pe.transpose(psTx[g % 2][:, 128 * t:128 * (t + 1)],
                                                znb[b][:, 128 * k:128 * (k + 1)], IDENT[:, :])
                        last.then_inc(p1_pe, 1)
                    else:
                        pe.sem_inc(p1_pe, 1)
                pe.wait_ge(p1_dv, V1 * c + 10)
                for nb in range(4):
                    bank = psA if nb % 2 == 0 else psB
                    if nb >= 2:
                        pe.wait_ge(p1_dv, V1 * c + 10 + (nb - 2) + 1)
                    elif c > 0:
                        pe.wait_ge(p1_dv, V1 * (c - 1) + 10 + (nb + 2) + 1)
                    if "ffn" not in skip:
                        for k in range(NK):
                            pe.matmul(bank[:, 0:512], tbuf[:, 128 * k:128 * (k + 1)],
                                      big_r[:, k, 512 * nb:512 * (nb + 1)], start=(k == 0), stop=False)
                        pe.matmul(bank[:, 0:512], onesb[0:1, 0:128],
                                  lnbrowb[0:1, 512 * nb:512 * (nb + 1)],
                                  start=False, stop=True).then_inc(p1_pe, 1)
                    else:
                        pe.sem_inc(p1_pe, 1)
                pe.wait_ge(p1_ac, A1 * c + 9)
                for g in range(4):
                    if g >= 2:
                        pe.wait_ge(p1_dv, V1 * c + 17 + (g - 2) + 1)
                    else:
                        pe.wait_ge(p1_dv, V1 * c + 6 + (g + 2) + 1)
                    if "t2" not in skip:
                        for t in range(4):
                            k = 4 * g + t
                            last = pe.transpose(psTx[g % 2][:, 128 * t:128 * (t + 1)],
                                                znb[b][:, 128 * k:128 * (k + 1)], IDENT[:, :])
                        last.then_inc(p1_pe, 1)
                    else:
                        pe.sem_inc(p1_pe, 1)

    if stop_after == "P1":
        dbg_sem = sem("dbgsem")
        with nc.Block() as B:
            @B.gpsimd
            def _(pl):
                pl.wait_ge(p1_pl, 16 * (1 + nch1))
            @B.sync
            def _(sp):
                if "dump" in skip:
                    sp.sem_inc(dbg_sem, 16)
                else:
                    sp.dma_start(dram["Ps"].ap().rearrange("(a p) n -> p a n", p=128),
                                 xn2T.rearrange("p (a n) -> p a n", a=16)).then_inc(dbg_sem, 16)
                sp.wait_ge(dbg_sem, 16)
        stack1.close()
        return nc, stack

    stack1.close()

    stack2 = contextlib.ExitStack()

    def sb2(name, shape, dtype):
        return stack2.enter_context(nc.sbuf_tensor(name, shape, dtype))

    qT = sb2("qT", [128, HPG * 128], BF16)
    kTb = sb2("kTb", [128, HPG * 128], BF16)
    qnat = sb2("qnat", [128, WG], BF16)
    knat = sb2("knat", [128, WG], BF16)
    k2 = sb2("k2", [128, HPG * 128], BF16)
    vaug = sb2("vaug", [128, HPG * 129], BF16)
    retb = sb2("retb", [128, 256], BF16)
    crosse = sb2("crosse", [128, 258], F32)
    ataug = sb2("ataug", [128, 258], F32)
    rsc = sb2("rsc", [128, 8], F32)
    state = [sb2(f"state{h}", [128, 129], F32) for h in range(HPG)]
    stateb = [sb2(f"stateb{h}", [128, 129], BF16) for h in range(HPG)]
    attnc = sb2("attnc", [128, WG], F32)
    gst = sb2("gst", [128, 32], F32)
    gsq = sb2("gsq", [128, WG], F32)
    Pbuf = sb2("Pbuf", [128, WG], BF16)
    Dloc = sb2("Dloc", [128, HPG * 128], F32)
    hgbias = sb2("hgbias", [1, 4 * WG], BF16)

    RQ, RK, RV, RG = 0, 8192, 16384, 24576
    PEB = ACB = 16
    PE2, AC2, DV2 = 9, 9, 17
    prev_pl = p1_pl
    prev_pl_cnt = 16 * (1 + NCH)
    for hg in range(NHG):
        g_pl = sem(f"g{hg}pl")
        g_pe = sem(f"g{hg}pe")
        g_ac = sem(f"g{hg}ac")
        g_dv = sem(f"g{hg}dv")
        g_st = sem(f"g{hg}st")
        cw = 512 * hg

        def wslice_dma(pl, wname, region):
            src = dram[wname].ap()[:, cw:cw + WG].rearrange("(k p) n -> p k n", p=128)
            dst = big[:, region:region + 8192].rearrange("p (k n) -> p k n", k=NK)
            pl.dma_start(dst, src).then_inc(g_pl, 16)

        with nc.Block() as B:
            @B.gpsimd
            def _(pl, hg=hg, g_pl=g_pl, g_pe=g_pe, cw=cw, prev_pl=prev_pl, prev_pl_cnt=prev_pl_cnt):
                if prev_pl is not None:
                    pl.wait_ge(prev_pl, prev_pl_cnt)
                for i, nm in enumerate(["bgR", "bkR", "bvR", "bqR"]):
                    pl.dma_start(hgbias[0:1, WG * i:WG * (i + 1)],
                                 dram[nm].ap()[0:1, cw:cw + WG]).then_inc(g_pl, 16)
                wslice_dma(pl, "Wg", RQ)
                wslice_dma(pl, "Wk", RK)
                wslice_dma(pl, "Wv", RV)
                pl.wait_ge(g_pe, NCH)
                wslice_dma(pl, "Wq", RQ)

            @B.tensor
            def _(pe, g_pl=g_pl, g_pe=g_pe, g_ac=g_ac, cw=cw):
                pe.wait_ge(g_pl, 80)
                for c in range(NCH):
                    bank = psA if c % 2 == 0 else psB
                    if c >= 2:
                        pe.wait_ge(g_ac, c - 1)
                    for k in range(NK):
                        pe.matmul(bank[:, 0:512], xn2T_r[:, k, 128 * c:128 * (c + 1)],
                                  big[:, RQ + 512 * k:RQ + 512 * (k + 1)], start=(k == 0), stop=False)
                    pe.matmul(bank[:, 0:512], onesb[0:1, 0:128], hgbias[0:1, 0:WG],
                              start=False, stop=True).then_inc(g_pe, 1)

            @B.scalar
            def _(ac, hg=hg, g_pe=g_pe, g_ac=g_ac):
                for h in range(HPG):
                    ac.activation(Dloc[:, 128 * h:128 * (h + 1)], DIFFT[:, :], AF.Exp,
                                  scale=decB[:, HPG * hg + h:HPG * hg + h + 1])
                for c in range(NCH):
                    bank = psA if c % 2 == 0 else psB
                    ac.wait_ge(g_pe, c + 1)
                    ac.activation(big[:, RG + 512 * c:RG + 512 * (c + 1)], bank[:, 0:512],
                                  AF.Silu).then_inc(g_ac, 1)

            @B.vector
            def _(dv):
                dv.memset(vaug[:, :], 1.0)
                for h in range(HPG):
                    dv.memset(state[h][:, :], 0.0)
                    dv.memset(stateb[h][:, :], 0.0)

        with nc.Block() as B:
            @B.tensor
            def _(pe, hg=hg, g_pl=g_pl, g_pe=g_pe, g_ac=g_ac, g_dv=g_dv, cw=cw):
                for c in range(NCH):
                    b = c % 2
                    if c == 0:
                        pe.wait_ge(g_pl, 96)
                    else:
                        pe.wait_ge(g_ac, ACB + AC2 * (c - 1) + 2)
                    for k in range(NK):
                        pe.matmul(psB[:, 0:512], xn2T_r[:, k, 128 * c:128 * (c + 1)],
                                  big[:, RK + 512 * k:RK + 512 * (k + 1)], start=(k == 0), stop=False)
                    pe.matmul(psB[:, 0:512], onesb[0:1, 0:128], hgbias[0:1, WG:2 * WG],
                              start=False, stop=True).then_inc(g_pe, 1)
                    if c == 0:
                        pe.wait_ge(g_pl, 112)
                    else:
                        pe.wait_ge(g_dv, DV2 * (c - 1) + 11)
                    for k in range(NK):
                        pe.matmul(psC[:, 0:512], xn2T_r[:, k, 128 * c:128 * (c + 1)],
                                  big[:, RV + 512 * k:RV + 512 * (k + 1)], start=(k == 0), stop=False)
                    pe.matmul(psC[:, 0:512], onesb[0:1, 0:128], hgbias[0:1, 2 * WG:3 * WG],
                              start=False, stop=True).then_inc(g_pe, 1)
                    if c == 0:
                        pe.wait_ge(g_pl, 128)
                    else:
                        pe.wait_ge(g_ac, ACB + AC2 * (c - 1) + 1)
                    for k in range(NK):
                        pe.matmul(psA[:, 0:512], xn2T_r[:, k, 128 * c:128 * (c + 1)],
                                  big[:, RQ + 512 * k:RQ + 512 * (k + 1)], start=(k == 0), stop=False)
                    pe.matmul(psA[:, 0:512], onesb[0:1, 0:128], hgbias[0:1, 3 * WG:4 * WG],
                              start=False, stop=True).then_inc(g_pe, 1)
                    pe.wait_ge(g_ac, ACB + AC2 * c + 1)
                    if c >= 1:
                        pe.wait_ge(g_dv, DV2 * (c - 1) + 3)
                    for t in range(4):
                        last = pe.transpose(psT0[:, 128 * t:128 * (t + 1)],
                                            qnat[:, 128 * t:128 * (t + 1)], IDENT[:, :])
                    last.then_inc(g_pe, 1)
                    pe.wait_ge(g_ac, ACB + AC2 * c + 2)
                    for t in range(4):
                        last = pe.transpose(psT1[:, 128 * t:128 * (t + 1)],
                                            knat[:, 128 * t:128 * (t + 1)], IDENT[:, :])
                    last.then_inc(g_pe, 1)
                    for pr in range(2):
                        h0, h1 = 2 * pr, 2 * pr + 1
                        pe.wait_ge(g_dv, DV2 * c + 3 + pr)
                        for j, hh in enumerate((h0, h1)):
                            last = pe.matmul(psS[:, 128 * j:128 * (j + 1)],
                                             kTb[:, 128 * hh:128 * (hh + 1)],
                                             qT[:, 128 * hh:128 * (hh + 1)], start=True, stop=True)
                        last.then_inc(g_pe, 1)
                        pe.wait_ge(g_dv, DV2 * c + 4 + 5 * pr)
                        if pr == 0:
                            if c >= 1:
                                pe.wait_ge(g_ac, ACB + AC2 * (c - 1) + 5)
                        else:
                            pe.wait_ge(g_ac, ACB + AC2 * c + 3)
                        for j, hh in enumerate((h0, h1)):
                            pe.matmul(psL[:, 129 * j:129 * (j + 1)], retb[:, 128 * j:128 * (j + 1)],
                                      vaug[:, 129 * hh:129 * (hh + 1)], start=True, stop=True)
                            pe.matmul(psX[:, 129 * j:129 * (j + 1)],
                                      qT[:, 128 * hh:128 * (hh + 1)],
                                      stateb[hh][:, :], start=True, stop=True)
                            last = pe.matmul(psU[:, 129 * j:129 * (j + 1)],
                                             k2[:, 128 * hh:128 * (hh + 1)],
                                             vaug[:, 129 * hh:129 * (hh + 1)], start=True, stop=True)
                        last.then_inc(g_pe, 1)

            @B.scalar
            def _(ac, hg=hg, g_pe=g_pe, g_ac=g_ac, g_dv=g_dv):
                for c in range(NCH):
                    b = c % 2
                    ac.wait_ge(g_pe, PEB + PE2 * c + 3)
                    ac.activation(qnat[:, :], psA[:, 0:512], AF.Identity, scale=qscb[:, 0:1]).then_inc(g_ac, 1)
                    ac.activation(knat[:, :], psB[:, 0:512], AF.Copy).then_inc(g_ac, 1)
                    for pr in range(2):
                        h0, h1 = 2 * pr, 2 * pr + 1
                        ac.wait_ge(g_pe, PEB + PE2 * c + 7 + 2 * pr)
                        for j, hh in enumerate((h0, h1)):
                            ac.activation(crosse[:, 129 * j:129 * (j + 1)],
                                          psX[:, 129 * j:129 * (j + 1)], AF.Identity,
                                          scale=erB[:, HPG * hg + hh:HPG * hg + hh + 1])
                        ac.drain(); ac.sem_inc(g_ac, 1)
                        ac.wait_ge(g_dv, DV2 * c + 6 + 5 * pr)
                        for hh in (h0, h1):
                            ac.activation(stateb[hh][:, :], state[hh][:, :], AF.Copy)
                        ac.drain(); ac.sem_inc(g_ac, 1)
                    ac.wait_ge(g_dv, DV2 * c + 13)
                    for h in range(HPG):
                        ac.activation(gsq[:, 128 * h:128 * (h + 1)], attnc[:, 128 * h:128 * (h + 1)],
                                      AF.Square, accum_out=gst[:, 4 + h:5 + h])
                    ac.drain(); ac.sem_inc(g_ac, 1)
                    ac.wait_ge(g_dv, DV2 * c + 15)
                    ac.activation(gst[:, 20:24], gst[:, 16:20], AF.Sqrt, bias=epsb[:, 0:1])
                    ac.drain(); ac.sem_inc(g_ac, 1)
                    ac.wait_ge(g_dv, DV2 * c + 16)
                    for h in range(HPG):
                        last = ac.activation(gsq[:, 128 * h:128 * (h + 1)], attnc[:, 128 * h:128 * (h + 1)],
                                             AF.Identity, scale=gst[:, 24 + h:25 + h],
                                             bias=gst[:, 28 + h:29 + h])
                    last.then_inc(g_ac, 1)

            @B.vector
            def _(dv, hg=hg, g_pe=g_pe, g_ac=g_ac, g_dv=g_dv, g_st=g_st, cw=cw):
                for c in range(NCH):
                    b = c % 2
                    dv.wait_ge(g_pe, PEB + PE2 * c + 2)
                    for h in range(HPG):
                        last = dv.tensor_copy(vaug[:, 129 * h:129 * h + 128], psC[:, 128 * h:128 * (h + 1)])
                    last.then_inc(g_dv, 1)
                    dv.wait_ge(g_ac, ACB + AC2 * c + 2)
                    for h in range(HPG):
                        last = dv.tensor_scalar(k2[:, 128 * h:128 * (h + 1)], knat[:, 128 * h:128 * (h + 1)],
                                                ekB[:, HPG * hg + h:HPG * hg + h + 1], None, OP.mult)
                    last.then_inc(g_dv, 1)
                    dv.wait_ge(g_pe, PEB + PE2 * c + 5)
                    dv.tensor_copy(qT[:, :], psT0[:, 0:512])
                    dv.tensor_copy(kTb[:, :], psT1[:, 0:512]).then_inc(g_dv, 1)
                    for pr in range(2):
                        h0, h1 = 2 * pr, 2 * pr + 1
                        dv.wait_ge(g_pe, PEB + PE2 * c + 6 + 2 * pr)
                        dv.tensor_tensor(retb[:, 0:256], psS[:, 0:256],
                                         Dloc[:, 256 * pr:256 * (pr + 1)], OP.mult).then_inc(g_dv, 1)
                        dv.wait_ge(g_ac, ACB + AC2 * c + 3 + 2 * pr)
                        dv.tensor_add(ataug[:, 0:258], crosse[:, 0:258],
                                      psL[:, 0:258]).then_inc(g_dv, 1)
                        dv.wait_ge(g_pe, PEB + PE2 * c + 7 + 2 * pr)
                        for hh in (h0, h1):
                            j = hh - 2 * pr
                            dv.scalar_tensor_tensor(state[hh][:, :], state[hh][:, :],
                                                    edC[:, HPG * hg + hh:HPG * hg + hh + 1],
                                                    psU[:, 129 * j:129 * (j + 1)], OP.mult, OP.add)
                        dv.drain(); dv.sem_inc(g_dv, 1)
                        for j in range(2):
                            dv.tensor_scalar(rsc[:, j:j + 1], ataug[:, 129 * j + 128:129 * j + 129],
                                             -1.0, None, OP.mult)
                        nop2(dv)
                        for j in range(2):
                            dv.tensor_tensor(rsc[:, 2 + j:3 + j], rsc[:, j:j + 1],
                                             ataug[:, 129 * j + 128:129 * j + 129], OP.max)
                        nop2(dv)
                        dv.tensor_scalar(rsc[:, 4:6], rsc[:, 2:4], 1.0, None, OP.max)
                        nop2(dv)
                        dv.reciprocal(rsc[:, 6:8], rsc[:, 4:6])
                        dv.drain(); dv.sem_inc(g_dv, 1)
                        if c >= 1 and pr == 0:
                            dv.wait_ge(g_ac, ACB + AC2 * (c - 1) + 9)
                        nop2(dv)
                        for j, hh in enumerate((h0, h1)):
                            last = dv.tensor_scalar(attnc[:, 128 * hh:128 * (hh + 1)],
                                                    ataug[:, 129 * j:129 * j + 128],
                                                    rsc[:, 6 + j:7 + j], None, OP.mult)
                        last.then_inc(g_dv, 1)
                    for h in range(HPG):
                        dv.reduce_sum(gst[:, h:h + 1], attnc[:, 128 * h:128 * (h + 1)], axis=AX)
                    dv.drain(); dv.sem_inc(g_dv, 1)
                    dv.wait_ge(g_ac, ACB + AC2 * c + 7)
                    nop2(dv)
                    dv.tensor_scalar(gst[:, 8:12], gst[:, 0:4], INV_HD, None, OP.mult)
                    nop2(dv)
                    dv.tensor_tensor(gst[:, 12:16], gst[:, 8:12], gst[:, 8:12], OP.mult)
                    nop2(dv)
                    dv.scalar_tensor_tensor(gst[:, 16:20], gst[:, 4:8], INV_HD, gst[:, 12:16],
                                            OP.mult, OP.subtract)
                    dv.drain(); dv.sem_inc(g_dv, 1)
                    dv.wait_ge(g_ac, ACB + AC2 * c + 8)
                    dv.reciprocal(gst[:, 24:28], gst[:, 20:24])
                    nop2(dv)
                    dv.scalar_tensor_tensor(gst[:, 28:32], gst[:, 8:12], -1.0, gst[:, 24:28],
                                            OP.mult, OP.mult)
                    dv.drain(); dv.sem_inc(g_dv, 1)
                    dv.wait_ge(g_ac, ACB + AC2 * c + 9)
                    if c >= 1:
                        dv.wait_ge(g_st, 16 * c)
                    dv.tensor_tensor(gsq[:, :], gsq[:, :], gnwB[:, cw:cw + WG], OP.mult)
                    dv.tensor_tensor(gsq[:, :], gsq[:, :], gnbB[:, cw:cw + WG], OP.add)
                    dv.tensor_tensor(Pbuf[:, :], gsq[:, :], big[:, RG + 512 * c:RG + 512 * (c + 1)],
                                     OP.mult).then_inc(g_dv, 1)

            @B.gpsimd
            def _(pl, g_dv=g_dv, g_st=g_st, cw=cw):
                for c in range(NCH):
                    pl.wait_ge(g_dv, DV2 * c + DV2)
                    pl.dma_start(dram["Ps"].ap()[128 * c:128 * (c + 1), cw:cw + WG],
                                 Pbuf[:, :]).then_inc(g_st, 16)

            @B.sync
            def _(sp, g_st=g_st):
                sp.wait_ge(g_st, 16 * NCH)

        prev_pl = g_pl
        prev_pl_cnt = 64

    if stop_after == "P2":
        dbg_sem = sem("dbgsem")
        with nc.Block() as B:
            @B.gpsimd
            def _(pl):
                pl.wait_ge(prev_pl, prev_pl_cnt)
            @B.sync
            def _(sp):
                sp.sem_inc(dbg_sem, 1)
                sp.wait_ge(dbg_sem, 1)
        stack2.close()
        return nc, stack

    stack2.close()

    stack3 = contextlib.ExitStack()

    def sb3(name, shape, dtype):
        return stack3.enter_context(nc.sbuf_tensor(name, shape, dtype))

    xb = [sb3(f"xc{i}", [128, E], F32) for i in range(2)]
    x1b = [sb3(f"x1c{i}", [128, E], F32) for i in range(2)]
    ub = sb3("uc", [128, E], F32)
    tbuf = sb3("tc", [128, E], BF16)
    boRb = sb3("boRb", [1, E], BF16)
    tbuf_r = tbuf.rearrange("p (k e) -> p k e", k=NK)

    q_ld = sem("q3ld")
    q_pl = sem("q3pl")
    q_pe = sem("q3pe")
    q_ac = sem("q3ac")
    q_dv = sem("q3dv")
    q_st = sem("q3st")
    P3, A3, V3 = 16, 6, 22
    with nc.Block() as B:
        @B.gpsimd
        def _(pl):
            pl.wait_ge(prev_pl, prev_pl_cnt)
            pl.dma_start(boRb[0:1, :], dram["boR"].ap()[:, :]).then_inc(q_pl, 16)
            src = dram["Wo"].ap().rearrange("(k p) n -> p k n", p=128)
            pl.dma_start(xn2T.rearrange("p (k n) -> p k n", k=NK), src).then_inc(q_pl, 16)
            for c in range(NCH):
                pl.wait_ge(q_dv, V3 * c + V3)
                pl.dma_start(dram["y"].ap()[128 * c:128 * (c + 1), :], xb[c % 2][:, :]).then_inc(q_st, 16)
            pl.wait_ge(q_st, 16 * NCH)

        @B.sync
        def _(sp):
            sp.wait_ge(prev_pl, prev_pl_cnt)
            sp.dma_start(big[:, :], dram["Wts"].ap()[:, :]).then_inc(q_ld, 16)
            for c in range(NCH):
                b = c % 2
                if c >= 2:
                    sp.wait_ge(q_pe, P3 * (c - 2) + 12)
                    sp.wait_ge(q_st, 16 * (c - 1))
                sp.dma_start(znb[b][:, :], dram["Ps"].ap()[128 * c:128 * (c + 1), :]).then_inc(q_ld, 16)
                sp.dma_start(xb[b][:, :], dram["x1s"].ap()[128 * c:128 * (c + 1), :]).then_inc(q_ld, 16)

        @B.tensor
        def _(pe):
            pe.wait_ge(q_pl, 32)
            for c in range(NCH):
                b = c % 2
                pe.wait_ge(q_ld, 16 * (2 * c + 2))
                for g in range(4):
                    if g >= 2:
                        pe.wait_ge(q_dv, V3 * c + (g - 2) + 1)
                    elif c > 0:
                        pe.wait_ge(q_dv, V3 * (c - 1) + 15 + (g + 2))
                    for t in range(4):
                        k = 4 * g + t
                        last = pe.transpose(psTx[g % 2][:, 128 * t:128 * (t + 1)],
                                            znb[b][:, 128 * k:128 * (k + 1)], IDENT[:, :])
                    last.then_inc(q_pe, 1)
                pe.wait_ge(q_dv, V3 * c + 4)
                for nb in range(4):
                    bank = psA if nb % 2 == 0 else psB
                    if nb >= 2:
                        pe.wait_ge(q_dv, V3 * c + 4 + (nb - 2) + 1)
                    elif c > 0:
                        pe.wait_ge(q_dv, V3 * (c - 1) + 19 + (nb + 2))
                    for k in range(NK):
                        pe.matmul(bank[:, 0:512], tbuf[:, 128 * k:128 * (k + 1)],
                                  xn2T_r[:, k, 512 * nb:512 * (nb + 1)], start=(k == 0), stop=False)
                    pe.matmul(bank[:, 0:512], onesb[0:1, 0:128], boRb[0:1, 512 * nb:512 * (nb + 1)],
                              start=False, stop=True).then_inc(q_pe, 1)
                pe.wait_ge(q_ac, A3 * c + 6)
                for g in range(4):
                    if g >= 2:
                        pe.wait_ge(q_dv, V3 * c + 15 + (g - 2))
                    else:
                        pe.wait_ge(q_dv, V3 * c + (g + 2) + 1)
                    for t in range(4):
                        k = 4 * g + t
                        last = pe.transpose(psTx[g % 2][:, 128 * t:128 * (t + 1)],
                                            znb[b][:, 128 * k:128 * (k + 1)], IDENT[:, :])
                    last.then_inc(q_pe, 1)
                pe.wait_ge(q_dv, V3 * c + 18)
                if c == 0:
                    pe.wait_ge(q_ld, 16)
                for nb in range(4):
                    bank = psA if nb % 2 == 0 else psB
                    if nb >= 2:
                        pe.wait_ge(q_dv, V3 * c + 18 + (nb - 2) + 1)
                    else:
                        pe.wait_ge(q_dv, V3 * c + 4 + (nb + 2) + 1)
                    for k in range(NK):
                        pe.matmul(bank[:, 0:512], tbuf[:, 128 * k:128 * (k + 1)],
                                  big_r[:, k, 512 * nb:512 * (nb + 1)], start=(k == 0), stop=False)
                    pe.matmul(bank[:, 0:512], onesb[0:1, 0:128],
                              lnbrowb[0:1, 512 * nb:512 * (nb + 1)],
                              start=False, stop=True).then_inc(q_pe, 1)

        @B.scalar
        def _(ac):
            for c in range(NCH):
                b = c % 2
                ac.wait_ge(q_dv, V3 * c + 8)
                ac.activation(ub[:, :], x1b[b][:, :], AF.Square,
                              accum_out=stt[b][:, 4:5])
                ac.drain(); ac.sem_inc(q_ac, 1)
                ac.wait_ge(q_dv, V3 * c + 10)
                ac.activation(stt[b][:, 12:13], stt[b][:, 10:11], AF.Sqrt, bias=epsb[:, 0:1])
                ac.drain(); ac.sem_inc(q_ac, 1)
                ac.wait_ge(q_dv, V3 * c + 11)
                ac.activation(ub[:, :], x1b[b][:, :], AF.Identity, scale=stt[b][:, 13:14],
                              bias=stt[b][:, 14:15]).then_inc(q_ac, 1)
                ac.wait_ge(q_pe, P3 * c + 4)
                ac.activation(znb[b][:, :], ub[:, :], AF.Square,
                              accum_out=stt[b][:, 20:21])
                ac.drain(); ac.sem_inc(q_ac, 1)
                ac.wait_ge(q_dv, V3 * c + 13)
                ac.activation(stt[b][:, 28:29], stt[b][:, 26:27], AF.Sqrt, bias=epsb[:, 0:1])
                ac.drain(); ac.sem_inc(q_ac, 1)
                ac.wait_ge(q_dv, V3 * c + 14)
                ac.activation(znb[b][:, :], ub[:, :], AF.Identity, scale=stt[b][:, 29:30],
                              bias=stt[b][:, 30:31]).then_inc(q_ac, 1)

        @B.vector
        def _(dv):
            for c in range(NCH):
                b = c % 2
                for g in range(4):
                    dv.wait_ge(q_pe, P3 * c + g + 1)
                    dv.tensor_copy(tbuf_r[:, 4 * g:4 * g + 4, :],
                                   psT_r[g % 2][:, 0:4, :]).then_inc(q_dv, 1)
                for nb in range(4):
                    dv.wait_ge(q_pe, P3 * c + 4 + nb + 1)
                    if nb == 0:
                        dv.wait_ge(q_ld, 16 * (2 * c + 3))
                    bank = psA if nb % 2 == 0 else psB
                    dv.tensor_add(x1b[b][:, 512 * nb:512 * (nb + 1)],
                                  xb[b][:, 512 * nb:512 * (nb + 1)],
                                  bank[:, 0:512]).then_inc(q_dv, 1)
                dv.reduce_sum(stt[b][:, 0:1], x1b[b][:, :], axis=AX)
                dv.drain(); dv.sem_inc(q_dv, 1)
                dv.wait_ge(q_ac, A3 * c + 1)
                stats3(dv, b, 0); dv.drain(); dv.sem_inc(q_dv, 1)
                dv.wait_ge(q_ac, A3 * c + 2)
                statsfin(dv, b, 0); dv.drain(); dv.sem_inc(q_dv, 1)
                dv.wait_ge(q_ac, A3 * c + 3)
                dv.reduce_sum(stt[b][:, 16:17], ub[:, :], axis=AX)
                dv.drain(); dv.sem_inc(q_dv, 1)
                dv.wait_ge(q_ac, A3 * c + 4)
                stats3(dv, b, 16); dv.drain(); dv.sem_inc(q_dv, 1)
                dv.wait_ge(q_ac, A3 * c + 5)
                statsfin(dv, b, 16); dv.drain(); dv.sem_inc(q_dv, 1)
                for g in range(4):
                    dv.wait_ge(q_pe, P3 * c + 8 + g + 1)
                    dv.tensor_copy(tbuf_r[:, 4 * g:4 * g + 4, :],
                                   psT_r[g % 2][:, 0:4, :]).then_inc(q_dv, 1)
                for nb in range(4):
                    dv.wait_ge(q_pe, P3 * c + 12 + nb + 1)
                    if nb == 0 and c >= 2:
                        dv.wait_ge(q_st, 16 * (c - 1))
                    bank = psA if nb % 2 == 0 else psB
                    dv.tensor_add(xb[b][:, 512 * nb:512 * (nb + 1)],
                                  x1b[b][:, 512 * nb:512 * (nb + 1)],
                                  bank[:, 0:512]).then_inc(q_dv, 1)

    stack3.close()
    return nc, stack


def _host_inputs(inputs, core):
    f = np.ascontiguousarray
    d = {
        "x": f(np.asarray(inputs["L0_attn_x"])[core]),
        "Wf": f(np.asarray(inputs["L0_ffn_W"])),
        "Wq": f(np.asarray(inputs["L0_attn_Wq"])),
        "Wk": f(np.asarray(inputs["L0_attn_Wk"])),
        "Wv": f(np.asarray(inputs["L0_attn_Wv"])),
        "Wg": f(np.asarray(inputs["L0_attn_Wg"])),
        "Wo": f(np.asarray(inputs["L0_attn_Wo"])),
        "lnwT": f(np.asarray(inputs["L0_ffn_ln_w"]).reshape(NK, 128).T),
        "lnbT": f(np.asarray(inputs["L0_ffn_ln_b"]).reshape(NK, 128).T),
        "bqR": f(np.asarray(inputs["L0_attn_bq"]).reshape(1, E)),
        "bkR": f(np.asarray(inputs["L0_attn_bk"]).reshape(1, E)),
        "bvR": f(np.asarray(inputs["L0_attn_bv"]).reshape(1, E)),
        "bgR": f(np.asarray(inputs["L0_attn_bg"]).reshape(1, E)),
        "boR": f(np.asarray(inputs["L0_attn_bo"]).reshape(1, E)),
        "gnwB": f(np.broadcast_to(np.asarray(inputs["L0_attn_gn_w"]), (128, E))),
        "gnbB": f(np.broadcast_to(np.asarray(inputs["L0_attn_gn_b"]), (128, E))),
        "decB": f(np.broadcast_to(np.asarray(inputs["L0_attn_decay"]), (128, H))),
    }
    ii = np.arange(128, dtype=np.float32)
    diff = ii[None, :] - ii[:, None]
    d["DIFFT"] = f(np.where(diff >= 0, diff, BIGNEG).astype(np.float32))
    d["ROWP1"] = f((ii + 1.0).reshape(128, 1))
    d["R127M"] = f((127.0 - ii).reshape(128, 1))
    d["IDENT"] = f(np.eye(128, dtype=ml_dtypes.bfloat16))
    return d


def kernel(**inputs):
    nc, stack = build(debug=False)
    in_maps = [_host_inputs(inputs, core) for core in range(8)]
    res = run_bass_kernel_spmd(nc, in_maps, core_ids=list(range(8)))
    out = np.stack([np.asarray(res.results[i]["y"]) for i in range(8)], axis=0)
    return np.ascontiguousarray(out.astype(np.float32, copy=False))


# version 03
# speedup vs baseline: 1.0212x; 1.0212x over previous
import contextlib
import numpy as np
import ml_dtypes
import concourse.bass as bass
import concourse.mybir as mybir
from concourse.bass_utils import run_bass_kernel_spmd

F32 = mybir.dt.float32
BF16 = mybir.dt.bfloat16
AF = mybir.ActivationFunctionType
OP = mybir.AluOpType
AX = mybir.AxisListType.X

S = 2048
E = 2048
H = 16
HD = 128
NCH = 16
NK = 16
NHG = 4
HPG = 4
WG = 512
EPS = 1e-5
QSCALE = float(HD) ** -0.5
INV_E = 1.0 / float(E)
INV_HD = 1.0 / float(HD)
BIGNEG = 1.0e9


def build(debug=False, stop_after=None, nch1=NCH, skip=()):
    nc = bass.Bass("TRN2", target_bir_lowering=False)
    dram = {}

    def dt(name, shape, dtype, kind):
        dram[name] = nc.dram_tensor(name, shape, dtype, kind=kind)

    dt("x", [S, E], F32, "ExternalInput")
    for nm in ["Wf", "Wq", "Wk", "Wv", "Wg", "Wo"]:
        dt(nm, [E, E], F32, "ExternalInput")
    dt("lnwT", [128, NK], F32, "ExternalInput")
    dt("lnbT", [128, NK], F32, "ExternalInput")
    for nm in ["bqR", "bkR", "bvR", "bgR", "boR"]:
        dt(nm, [1, E], F32, "ExternalInput")
    dt("gnwB", [128, E], F32, "ExternalInput")
    dt("gnbB", [128, E], F32, "ExternalInput")
    dt("decB", [128, H], F32, "ExternalInput")
    dt("DIFFT", [128, 128], F32, "ExternalInput")
    dt("ROWP1", [128, 1], F32, "ExternalInput")
    dt("R127M", [128, 1], F32, "ExternalInput")
    dt("IDENT", [128, 128], BF16, "ExternalInput")
    dt("y", [S, E], F32, "ExternalOutput")
    sk = "ExternalOutput" if debug else "Internal"
    dt("x1s", [S, E], F32, sk)
    dt("Ps", [S, E], BF16, sk)
    dt("Wts", [128, NK * E], BF16, sk)

    stack = contextlib.ExitStack()

    def sb(name, shape, dtype):
        return stack.enter_context(nc.sbuf_tensor(name, shape, dtype))

    def psum(name, shape, dtype):
        return stack.enter_context(nc.psum_tensor(name, shape, dtype))

    def sem(name):
        return stack.enter_context(nc.semaphore(name))

    xn2T = sb("xn2T", [128, NK * E], BF16)
    big = sb("big", [128, NK * E], BF16)
    znb = [sb(f"znb{i}", [128, E], BF16) for i in range(2)]
    gnwB = sb("gnwBs", [128, E], F32)
    gnbB = sb("gnbBs", [128, E], F32)
    lnwT = sb("lnwTs", [128, NK], F32)
    lnbT = sb("lnbTs", [128, NK], F32)
    decB = sb("decBs", [128, H], F32)
    DIFFT = sb("DIFFTs", [128, 128], F32)
    ROWP1 = sb("ROWP1s", [128, 1], F32)
    R127M = sb("R127Ms", [128, 1], F32)
    IDENT = sb("IDENTs", [128, 128], BF16)
    onesb = sb("onesb", [1, 128], BF16)
    onesf = sb("onesf", [128, 1], F32)
    onesrf = sb("onesrf", [1, 128], F32)
    lnbrowb = sb("lnbrowb", [1, E], BF16)
    halflnw = sb("halflnw", [128, NK], F32)
    lnbTb = sb("lnbTb", [128, NK], BF16)
    erB = sb("erB", [128, H], F32)
    ekB = sb("ekB", [128, H], F32)
    edC = sb("edC", [128, H], F32)
    stt = [sb(f"stt{i}", [128, 48], F32) for i in range(2)]
    wsc = sb("wsc", [128, 1], F32)
    scl1 = sb("scl1", [1, 1], F32)
    epsb = sb("epsb", [128, 1], F32)
    qscb = sb("qscb", [128, 1], F32)
    c128b = sb("c128b", [128, 1], F32)
    psA = psum("psA", [128, 512], F32)
    psB = psum("psB", [128, 512], F32)
    psC = psum("psC", [128, 512], F32)
    psS = psum("psS", [128, 512], F32)
    psL = psum("psL", [128, 512], F32)
    psX = psum("psX", [128, 512], F32)
    psT0 = psum("psT0", [128, 1024], BF16)
    psT1 = psum("psT1", [128, 1024], BF16)
    psTx = [psT0, psT1]
    psU = psC

    xn2T_r = xn2T.rearrange("p (k e) -> p k e", k=NK)
    big_r = big.rearrange("p (k e) -> p k e", k=NK)
    psT_r = [t.rearrange("p (k e) -> p k e", k=8) for t in psTx]

    def nop2(dv):
        dv.drain()

    def stats3(dv, b, base):
        nop2(dv)
        dv.tensor_scalar(stt[b][:, base + 8:base + 9], stt[b][:, base + 0:base + 1], INV_E, None, OP.mult)
        nop2(dv)
        dv.tensor_tensor(stt[b][:, base + 9:base + 10], stt[b][:, base + 8:base + 9],
                         stt[b][:, base + 8:base + 9], OP.mult)
        nop2(dv)
        return dv.scalar_tensor_tensor(stt[b][:, base + 10:base + 11], stt[b][:, base + 4:base + 5], INV_E,
                                       stt[b][:, base + 9:base + 10], OP.mult, OP.subtract)

    def statsfin(dv, b, base):
        dv.reciprocal(stt[b][:, base + 13:base + 14], stt[b][:, base + 12:base + 13])
        nop2(dv)
        return dv.scalar_tensor_tensor(stt[b][:, base + 14:base + 15], stt[b][:, base + 8:base + 9], -1.0,
                                       stt[b][:, base + 13:base + 14], OP.mult, OP.mult)

    stack1 = contextlib.ExitStack()

    def sb1(name, shape, dtype):
        return stack1.enter_context(nc.sbuf_tensor(name, shape, dtype))

    xb = [sb1(f"xb{i}", [128, E], F32) for i in range(2)]
    x1b = [sb1(f"x1b{i}", [128, E], F32) for i in range(2)]
    ub = sb1("ub", [128, E], F32)
    tbuf = sb1("tbuf", [128, E], BF16)
    tbuf_r = tbuf.rearrange("p (k e) -> p k e", k=NK)

    s0_ld = sem("s0ld")
    s0_w = [sem(f"s0w{i}") for i in range(4)]
    s0_ac = sem("s0ac")
    s0_dv = sem("s0dv")
    s0_pe = sem("s0pe")

    with nc.Block() as B:
        @B.sync
        def _(sp):
            small = [("lnwT", lnwT), ("lnbT", lnbT), ("gnwB", gnwB), ("gnbB", gnbB),
                     ("decB", decB), ("DIFFT", DIFFT), ("ROWP1", ROWP1),
                     ("R127M", R127M), ("IDENT", IDENT)]
            for nm, buf in small:
                sp.dma_start(buf[:, :], dram[nm].ap()[:, :]).then_inc(s0_ld, 16)
            for k in range(NK):
                if k >= 4:
                    sp.wait_ge(s0_ac, k - 3)
                buf = [xb[0], xb[1], x1b[0], x1b[1]][k % 4]
                sp.dma_start(buf[:, :], dram["Wf"].ap()[128 * k:128 * (k + 1), :]).then_inc(s0_w[k % 4], 16)
            sp.wait_ge(s0_ld, 16 * 9)

        @B.scalar
        def _(ac):
            for k in range(NK):
                ac.wait_ge(s0_w[k % 4], 16 * (k // 4 + 1))
                buf = [xb[0], xb[1], x1b[0], x1b[1]][k % 4]
                ac.activation(ub[:, :], buf[:, :], AF.Abs,
                              accum_out=stt[0][:, k:k + 1])
                ac.drain(); ac.sem_inc(s0_ac, 1)

        @B.vector
        def _(dv):
            dv.memset(onesb[:, :], 1.0)
            dv.memset(onesf[:, :], 1.0)
            dv.memset(onesrf[:, :], 1.0)
            dv.memset(epsb[:, :], EPS)
            dv.memset(qscb[:, :], QSCALE)
            dv.memset(c128b[:, :], 128.0)
            dv.wait_ge(s0_ac, NK)
            dv.reduce_sum(wsc[:, 0:1], stt[0][:, 0:NK], axis=AX)
            dv.drain(); dv.sem_inc(s0_dv, 1)

        @B.tensor
        def _(pe):
            pe.wait_ge(s0_dv, 1)
            pe.matmul(psA[0:1, 0:1], wsc[:, 0:1], onesf[:, 0:1],
                      start=True, stop=True).then_inc(s0_pe, 1)

        @B.gpsimd
        def _(pl):
            pass

    with nc.Block() as B:
        @B.vector
        def _(dv):
            dv.tensor_scalar(scl1[0:1, 0:1], psA[0:1, 0:1], 0.5 * INV_E * INV_E, None, OP.mult)

    s0_bc = sem("s0bc")
    with nc.Block() as B:
        @B.vector
        def _(dv):
            dv.tensor_scalar(onesrf[0:1, 0:128], onesrf[0:1, 0:128], scl1[0:1, 0:1], None, OP.mult)
            dv.tensor_scalar(halflnw[:, :], lnwT[:, :], 0.5, None, OP.mult)
            dv.tensor_copy(lnbTb[:, :], lnbT[:, :])

        @B.sync
        def _(sp):
            pass
    with nc.Block() as B:
        @B.sync
        def _(sp):
            sp.dma_start(dram["x1s"].ap()[0:1, 0:128], onesrf[0:1, 0:128]).then_inc(s0_bc, 16)
            sp.wait_ge(s0_bc, 16)
            src = dram["x1s"].ap()[0:1, 0:128].rearrange("a b -> b a")
            sp.dma_start(wsc[:, 0:1], src).then_inc(s0_bc, 16)
            sp.wait_ge(s0_bc, 32)
        @B.vector
        def _(dv):
            dv.wait_ge(s0_bc, 32)
            dv.memset(onesrf[:, :], 1.0)

        @B.scalar
        def _(ac):
            for h in range(H):
                ac.activation(erB[:, h:h + 1], ROWP1[:, 0:1], AF.Exp, scale=decB[:, h:h + 1])
                ac.activation(ekB[:, h:h + 1], R127M[:, 0:1], AF.Exp, scale=decB[:, h:h + 1])
            ac.activation(edC[:, :], decB[:, :], AF.Exp, scale=c128b[:, 0:1])

    s2_ld = [sem(f"s2ld{i}") for i in range(2)]
    s2_ac = sem("s2ac")
    s2_dv = sem("s2dv")
    s2_pe = sem("s2pe")
    with nc.Block() as B:
        @B.sync
        def _(sp):
            for k in range(NK):
                if k >= 2:
                    sp.wait_ge(s2_ac, 2 * (k - 1))
                sp.dma_start(xb[k % 2][:, :], dram["Wf"].ap()[128 * k:128 * (k + 1), :]).then_inc(s2_ld[k % 2], 16)

        @B.scalar
        def _(ac):
            for k in range(NK):
                ac.wait_ge(s2_ld[k % 2], 16 * (k // 2 + 1))
                if k >= 2:
                    ac.wait_ge(s2_pe, k - 1)
                    ac.wait_ge(s2_dv, 2 * (k - 1))
                ac.activation(znb[k % 2][:, :], xb[k % 2][:, :], AF.Sign).then_inc(s2_ac, 1)
                ac.activation(x1b[k % 2][:, :], xb[k % 2][:, :], AF.Abs).then_inc(s2_ac, 1)

        @B.vector
        def _(dv):
            for k in range(NK):
                dv.wait_ge(s2_ac, 2 * (k + 1))
                dv.scalar_tensor_tensor(znb[k % 2][:, :], x1b[k % 2][:, :], wsc[:, 0:1],
                                        znb[k % 2][:, :], OP.is_ge, OP.mult).then_inc(s2_dv, 1)
                dv.tensor_scalar(big_r[:, k, :], znb[k % 2][:, :],
                                 halflnw[:, k:k + 1], None, OP.mult).then_inc(s2_dv, 1)

        @B.tensor
        def _(pe):
            pbank = [psA, psB, psC, psS]
            for k in range(NK):
                pe.wait_ge(s2_dv, 2 * k + 1)
                last = None
                for nb in range(4):
                    last = pe.matmul(pbank[nb][0:1, 0:512], lnbTb[:, k:k + 1],
                                     znb[k % 2][:, 512 * nb:512 * (nb + 1)],
                                     start=(k == 0), stop=(k == NK - 1))
                last.then_inc(s2_pe, 1)

    with nc.Block() as B:
        @B.vector
        def _(dv):
            pbank = [psA, psB, psC, psS]
            for nb in range(4):
                dv.tensor_scalar(lnbrowb[0:1, 512 * nb:512 * (nb + 1)], pbank[nb][0:1, 0:512],
                                 0.5, None, OP.mult)

    if stop_after == "P0":
        dbg_sem = sem("dbgsem")
        with nc.Block() as B:
            @B.sync
            def _(sp):
                sp.dma_start(dram["Wts"].ap()[:, :], big[:, :]).then_inc(dbg_sem, 16)
                with nc.allow_non_contiguous_dma(reason="debug"):
                    sp.dma_start(dram["x1s"].ap()[0:128, 0:1], wsc[:, 0:1]).then_inc(dbg_sem, 16)
                    sp.dma_start(dram["x1s"].ap()[0:128, 8:32], stt[0][:, :]).then_inc(dbg_sem, 16)
                    sp.dma_start(dram["x1s"].ap()[0:1, 40:41], scl1[0:1, 0:1]).then_inc(dbg_sem, 16)
                    sp.dma_start(dram["x1s"].ap()[0:128, 50:51], onesf[:, 0:1]).then_inc(dbg_sem, 16)
                sp.wait_ge(dbg_sem, 80)
        stack1.close()
        return nc, stack

    p1_ld = [sem(f"p1ld{i}") for i in range(2)]
    p1_sp = sem("p1sp")
    p1_st = [sem(f"p1st{i}") for i in range(2)]
    p1_ac = sem("p1ac")
    p1_dv = sem("p1dv")
    p1_pe = sem("p1pe")
    A1, V1, P1 = 9, 21, 12

    with nc.Block() as B:
        @B.sync
        def _(sp):
            for c in range(nch1):
                if c >= 2:
                    sp.wait_ge(p1_dv, V1 * (c - 2) + 14)
                sp.dma_start(xb[c % 2][:, :], dram["x"].ap()[128 * c:128 * (c + 1), :]).then_inc(p1_ld[c % 2], 16)

        @B.gpsimd
        def _(pl):
            if "wts" in skip:
                pl.sem_inc(p1_sp, 16)
            else:
                pl.dma_start(dram["Wts"].ap()[:, :], big[:, :]).then_inc(p1_sp, 16)
            for c in range(nch1):
                pl.wait_ge(p1_dv, V1 * c + 14)
                if "stores" in skip:
                    pl.sem_inc(p1_st[c % 2], 16)
                else:
                    pl.dma_start(dram["x1s"].ap()[128 * c:128 * (c + 1), :],
                                 x1b[c % 2][:, :]).then_inc(p1_st[c % 2], 16)

        @B.scalar
        def _(ac):
            for c in range(nch1):
                b = c % 2
                ac.wait_ge(p1_ld[c % 2], 16 * (c // 2 + 1))
                if c >= 2:
                    ac.wait_ge(p1_st[c % 2], 16 * (c // 2))
                    ac.wait_ge(p1_dv, V1 * (c - 2) + V1)
                ac.activation(x1b[b][:, :], xb[b][:, :], AF.Square,
                              accum_out=stt[b][:, 4:5])
                ac.drain(); ac.sem_inc(p1_ac, 1)
                ac.wait_ge(p1_dv, V1 * c + 2)
                if "sqrt" in skip:
                    ac.activation(stt[b][:, 12:13], stt[b][:, 10:11], AF.Sqrt).then_inc(p1_ac, 1)
                else:
                    ac.activation(stt[b][:, 12:13], stt[b][:, 10:11], AF.Sqrt, bias=epsb[:, 0:1])
                ac.drain(); ac.sem_inc(p1_ac, 1)
                ac.wait_ge(p1_dv, V1 * c + 3)
                ac.activation(ub[:, :], xb[b][:, :], AF.Identity, scale=stt[b][:, 13:14],
                              bias=stt[b][:, 14:15]).then_inc(p1_ac, 1)
                ac.activation(x1b[b][:, :], ub[:, :], AF.Square,
                              accum_out=stt[b][:, 20:21])
                ac.drain(); ac.sem_inc(p1_ac, 1)
                ac.wait_ge(p1_dv, V1 * c + 5)
                ac.activation(stt[b][:, 28:29], stt[b][:, 26:27], AF.Sqrt, bias=epsb[:, 0:1])
                ac.drain(); ac.sem_inc(p1_ac, 1)
                ac.wait_ge(p1_dv, V1 * c + 6)
                ac.activation(znb[b][:, :], ub[:, :], AF.Identity, scale=stt[b][:, 29:30],
                              bias=stt[b][:, 30:31]).then_inc(p1_ac, 1)
                ac.wait_ge(p1_dv, V1 * c + 14)
                ac.activation(ub[:, :], x1b[b][:, :], AF.Square,
                              accum_out=stt[b][:, 36:37])
                ac.drain(); ac.sem_inc(p1_ac, 1)
                ac.wait_ge(p1_dv, V1 * c + 16)
                ac.activation(stt[b][:, 44:45], stt[b][:, 42:43], AF.Sqrt, bias=epsb[:, 0:1])
                ac.drain(); ac.sem_inc(p1_ac, 1)
                ac.wait_ge(p1_dv, V1 * c + 17)
                ac.activation(znb[b][:, :], x1b[b][:, :], AF.Identity, scale=stt[b][:, 45:46],
                              bias=stt[b][:, 46:47]).then_inc(p1_ac, 1)

        @B.vector
        def _(dv):
            for c in range(nch1):
                b = c % 2
                dv.wait_ge(p1_ld[c % 2], 16 * (c // 2 + 1))
                if c >= 2:
                    dv.wait_ge(p1_ac, A1 * (c - 2) + A1)
                dv.reduce_sum(stt[b][:, 0:1], xb[b][:, :], axis=AX)
                dv.drain(); dv.sem_inc(p1_dv, 1)
                dv.wait_ge(p1_ac, A1 * c + 1)
                stats3(dv, b, 0); dv.drain(); dv.sem_inc(p1_dv, 1)
                dv.wait_ge(p1_ac, A1 * c + 2)
                statsfin(dv, b, 0); dv.drain(); dv.sem_inc(p1_dv, 1)
                dv.wait_ge(p1_ac, A1 * c + 3)
                dv.reduce_sum(stt[b][:, 16:17], ub[:, :], axis=AX)
                dv.drain(); dv.sem_inc(p1_dv, 1)
                dv.wait_ge(p1_ac, A1 * c + 4)
                stats3(dv, b, 16); dv.drain(); dv.sem_inc(p1_dv, 1)
                dv.wait_ge(p1_ac, A1 * c + 5)
                statsfin(dv, b, 16); dv.drain(); dv.sem_inc(p1_dv, 1)
                for g in range(4):
                    dv.wait_ge(p1_pe, P1 * c + g + 1)
                    if "t1" not in skip:
                        dv.tensor_copy(tbuf_r[:, 4 * g:4 * g + 4, :],
                                       psT_r[g % 2][:, 0:4, :]).then_inc(p1_dv, 1)
                    else:
                        dv.sem_inc(p1_dv, 1)
                for nb in range(4):
                    dv.wait_ge(p1_pe, P1 * c + 4 + nb + 1)
                    bank = psA if nb % 2 == 0 else psB
                    if "ffn" in skip:
                        dv.tensor_copy(x1b[b][:, 512 * nb:512 * (nb + 1)],
                                       xb[b][:, 512 * nb:512 * (nb + 1)]).then_inc(p1_dv, 1)
                    else:
                        dv.tensor_add(x1b[b][:, 512 * nb:512 * (nb + 1)],
                                      xb[b][:, 512 * nb:512 * (nb + 1)],
                                      bank[:, 0:512]).then_inc(p1_dv, 1)
                dv.reduce_sum(stt[b][:, 32:33], x1b[b][:, :], axis=AX)
                dv.drain(); dv.sem_inc(p1_dv, 1)
                dv.wait_ge(p1_ac, A1 * c + 7)
                stats3(dv, b, 32); dv.drain(); dv.sem_inc(p1_dv, 1)
                dv.wait_ge(p1_ac, A1 * c + 8)
                statsfin(dv, b, 32); dv.drain(); dv.sem_inc(p1_dv, 1)
                for g in range(4):
                    dv.wait_ge(p1_pe, P1 * c + 8 + g + 1)
                    if "t2" not in skip:
                        dv.tensor_copy(xn2T_r[:, 4 * g:4 * g + 4, 128 * c:128 * (c + 1)],
                                       psT_r[g % 2][:, 0:4, :]).then_inc(p1_dv, 1)
                    else:
                        dv.sem_inc(p1_dv, 1)

        @B.tensor
        def _(pe):
            for c in range(nch1):
                b = c % 2
                pe.wait_ge(p1_ac, A1 * c + 6)
                for g in range(4):
                    if g >= 2:
                        pe.wait_ge(p1_dv, V1 * c + 6 + (g - 2) + 1)
                    elif c > 0:
                        pe.wait_ge(p1_dv, V1 * (c - 1) + 17 + (g + 2) + 1)
                    if "t1" not in skip:
                        for t in range(4):
                            k = 4 * g + t
                            last = pe.transpose(psTx[g % 2][:, 128 * t:128 * (t + 1)],
                                                znb[b][:, 128 * k:128 * (k + 1)], IDENT[:, :])
                        last.then_inc(p1_pe, 1)
                    else:
                        pe.sem_inc(p1_pe, 1)
                pe.wait_ge(p1_dv, V1 * c + 10)
                for nb in range(4):
                    bank = psA if nb % 2 == 0 else psB
                    if nb >= 2:
                        pe.wait_ge(p1_dv, V1 * c + 10 + (nb - 2) + 1)
                    elif c > 0:
                        pe.wait_ge(p1_dv, V1 * (c - 1) + 10 + (nb + 2) + 1)
                    if "ffn" not in skip:
                        for k in range(NK):
                            pe.matmul(bank[:, 0:512], tbuf[:, 128 * k:128 * (k + 1)],
                                      big_r[:, k, 512 * nb:512 * (nb + 1)], start=(k == 0), stop=False)
                        pe.matmul(bank[:, 0:512], onesb[0:1, 0:128],
                                  lnbrowb[0:1, 512 * nb:512 * (nb + 1)],
                                  start=False, stop=True).then_inc(p1_pe, 1)
                    else:
                        pe.sem_inc(p1_pe, 1)
                pe.wait_ge(p1_ac, A1 * c + 9)
                for g in range(4):
                    if g >= 2:
                        pe.wait_ge(p1_dv, V1 * c + 17 + (g - 2) + 1)
                    else:
                        pe.wait_ge(p1_dv, V1 * c + 6 + (g + 2) + 1)
                    if "t2" not in skip:
                        for t in range(4):
                            k = 4 * g + t
                            last = pe.transpose(psTx[g % 2][:, 128 * t:128 * (t + 1)],
                                                znb[b][:, 128 * k:128 * (k + 1)], IDENT[:, :])
                        last.then_inc(p1_pe, 1)
                    else:
                        pe.sem_inc(p1_pe, 1)

    if stop_after == "P1":
        dbg_sem = sem("dbgsem")
        with nc.Block() as B:
            @B.gpsimd
            def _(pl):
                pl.wait_ge(p1_sp, 16)
                for i in range(2):
                    pl.wait_ge(p1_st[i], 16 * (nch1 // 2))
            @B.sync
            def _(sp):
                if "dump" in skip:
                    sp.sem_inc(dbg_sem, 16)
                else:
                    sp.dma_start(dram["Ps"].ap().rearrange("(a p) n -> p a n", p=128),
                                 xn2T.rearrange("p (a n) -> p a n", a=16)).then_inc(dbg_sem, 16)
                sp.wait_ge(dbg_sem, 16)
        stack1.close()
        return nc, stack

    stack1.close()

    stack2 = contextlib.ExitStack()

    def sb2(name, shape, dtype):
        return stack2.enter_context(nc.sbuf_tensor(name, shape, dtype))

    qT = sb2("qT", [128, HPG * 128], BF16)
    kTb = sb2("kTb", [128, HPG * 128], BF16)
    qnat = sb2("qnat", [128, WG], BF16)
    knat = sb2("knat", [128, WG], BF16)
    k2 = sb2("k2", [128, HPG * 128], BF16)
    vaug = sb2("vaug", [128, HPG * 129], BF16)
    retb = sb2("retb", [128, 256], BF16)
    crosse = sb2("crosse", [128, 258], F32)
    ataug = sb2("ataug", [128, 258], F32)
    rsc = sb2("rsc", [128, 8], F32)
    state = [sb2(f"state{h}", [128, 129], F32) for h in range(HPG)]
    stateb = [sb2(f"stateb{h}", [128, 129], BF16) for h in range(HPG)]
    attnc = sb2("attnc", [128, WG], F32)
    gst = sb2("gst", [128, 32], F32)
    gsq = sb2("gsq", [128, WG], F32)
    Pbuf = sb2("Pbuf", [128, WG], BF16)
    Dloc = sb2("Dloc", [128, HPG * 128], F32)
    hgbias = sb2("hgbias", [1, 4 * WG], BF16)

    RQ, RK, RV, RG = 0, 8192, 16384, 24576
    PEB = ACB = 16
    PE2, AC2, DV2 = 9, 9, 17
    prev_pl = [(p1_sp, 16), (p1_st[0], 16 * (NCH // 2)), (p1_st[1], 16 * (NCH // 2))]
    for hg in range(NHG):
        g_bias = sem(f"g{hg}bias")
        g_wg = sem(f"g{hg}wg")
        g_wk = sem(f"g{hg}wk")
        g_wv = sem(f"g{hg}wv")
        g_wq = sem(f"g{hg}wq")
        g_pe = sem(f"g{hg}pe")
        g_ac = sem(f"g{hg}ac")
        g_dv = sem(f"g{hg}dv")
        g_st = sem(f"g{hg}st")
        cw = 512 * hg

        def wslice_dma(pl, wname, region, wsem):
            src = dram[wname].ap()[:, cw:cw + WG].rearrange("(k p) n -> p k n", p=128)
            dst = big[:, region:region + 8192].rearrange("p (k n) -> p k n", k=NK)
            pl.dma_start(dst, src).then_inc(wsem, 16)

        with nc.Block() as B:
            @B.gpsimd
            def _(pl, hg=hg, g_pe=g_pe, cw=cw, prev_pl=prev_pl,
                  g_bias=g_bias, g_wg=g_wg, g_wk=g_wk, g_wv=g_wv, g_wq=g_wq):
                if prev_pl is not None:
                    for ps_, pv_ in prev_pl:
                        pl.wait_ge(ps_, pv_)
                for i, nm in enumerate(["bgR", "bkR", "bvR", "bqR"]):
                    pl.dma_start(hgbias[0:1, WG * i:WG * (i + 1)],
                                 dram[nm].ap()[0:1, cw:cw + WG]).then_inc(g_bias, 16)
                wslice_dma(pl, "Wg", RQ, g_wg)
                wslice_dma(pl, "Wk", RK, g_wk)
                wslice_dma(pl, "Wv", RV, g_wv)
                pl.wait_ge(g_pe, NCH)
                wslice_dma(pl, "Wq", RQ, g_wq)

            @B.tensor
            def _(pe, g_pe=g_pe, g_ac=g_ac, cw=cw, g_bias=g_bias, g_wg=g_wg):
                pe.wait_ge(g_bias, 64)
                pe.wait_ge(g_wg, 16)
                for c in range(NCH):
                    bank = psA if c % 2 == 0 else psB
                    if c >= 2:
                        pe.wait_ge(g_ac, c - 1)
                    for k in range(NK):
                        pe.matmul(bank[:, 0:512], xn2T_r[:, k, 128 * c:128 * (c + 1)],
                                  big[:, RQ + 512 * k:RQ + 512 * (k + 1)], start=(k == 0), stop=False)
                    pe.matmul(bank[:, 0:512], onesb[0:1, 0:128], hgbias[0:1, 0:WG],
                              start=False, stop=True).then_inc(g_pe, 1)

            @B.scalar
            def _(ac, hg=hg, g_pe=g_pe, g_ac=g_ac):
                for h in range(HPG):
                    ac.activation(Dloc[:, 128 * h:128 * (h + 1)], DIFFT[:, :], AF.Exp,
                                  scale=decB[:, HPG * hg + h:HPG * hg + h + 1])
                for c in range(NCH):
                    bank = psA if c % 2 == 0 else psB
                    ac.wait_ge(g_pe, c + 1)
                    ac.activation(big[:, RG + 512 * c:RG + 512 * (c + 1)], bank[:, 0:512],
                                  AF.Silu).then_inc(g_ac, 1)

            @B.vector
            def _(dv):
                dv.memset(vaug[:, :], 1.0)
                for h in range(HPG):
                    dv.memset(state[h][:, :], 0.0)
                    dv.memset(stateb[h][:, :], 0.0)

        with nc.Block() as B:
            @B.tensor
            def _(pe, hg=hg, g_pe=g_pe, g_ac=g_ac, g_dv=g_dv, cw=cw,
                  g_wk=g_wk, g_wv=g_wv, g_wq=g_wq):
                for c in range(NCH):
                    b = c % 2
                    if c == 0:
                        pe.wait_ge(g_wk, 16)
                    else:
                        pe.wait_ge(g_ac, ACB + AC2 * (c - 1) + 2)
                    for k in range(NK):
                        pe.matmul(psB[:, 0:512], xn2T_r[:, k, 128 * c:128 * (c + 1)],
                                  big[:, RK + 512 * k:RK + 512 * (k + 1)], start=(k == 0), stop=False)
                    pe.matmul(psB[:, 0:512], onesb[0:1, 0:128], hgbias[0:1, WG:2 * WG],
                              start=False, stop=True).then_inc(g_pe, 1)
                    if c == 0:
                        pe.wait_ge(g_wv, 16)
                    else:
                        pe.wait_ge(g_dv, DV2 * (c - 1) + 11)
                    for k in range(NK):
                        pe.matmul(psC[:, 0:512], xn2T_r[:, k, 128 * c:128 * (c + 1)],
                                  big[:, RV + 512 * k:RV + 512 * (k + 1)], start=(k == 0), stop=False)
                    pe.matmul(psC[:, 0:512], onesb[0:1, 0:128], hgbias[0:1, 2 * WG:3 * WG],
                              start=False, stop=True).then_inc(g_pe, 1)
                    if c == 0:
                        pe.wait_ge(g_wq, 16)
                    else:
                        pe.wait_ge(g_ac, ACB + AC2 * (c - 1) + 1)
                    for k in range(NK):
                        pe.matmul(psA[:, 0:512], xn2T_r[:, k, 128 * c:128 * (c + 1)],
                                  big[:, RQ + 512 * k:RQ + 512 * (k + 1)], start=(k == 0), stop=False)
                    pe.matmul(psA[:, 0:512], onesb[0:1, 0:128], hgbias[0:1, 3 * WG:4 * WG],
                              start=False, stop=True).then_inc(g_pe, 1)
                    pe.wait_ge(g_ac, ACB + AC2 * c + 1)
                    if c >= 1:
                        pe.wait_ge(g_dv, DV2 * (c - 1) + 3)
                    for t in range(4):
                        last = pe.transpose(psT0[:, 128 * t:128 * (t + 1)],
                                            qnat[:, 128 * t:128 * (t + 1)], IDENT[:, :])
                    last.then_inc(g_pe, 1)
                    pe.wait_ge(g_ac, ACB + AC2 * c + 2)
                    for t in range(4):
                        last = pe.transpose(psT1[:, 128 * t:128 * (t + 1)],
                                            knat[:, 128 * t:128 * (t + 1)], IDENT[:, :])
                    last.then_inc(g_pe, 1)
                    for pr in range(2):
                        h0, h1 = 2 * pr, 2 * pr + 1
                        pe.wait_ge(g_dv, DV2 * c + 3 + pr)
                        for j, hh in enumerate((h0, h1)):
                            last = pe.matmul(psS[:, 128 * j:128 * (j + 1)],
                                             kTb[:, 128 * hh:128 * (hh + 1)],
                                             qT[:, 128 * hh:128 * (hh + 1)], start=True, stop=True)
                        last.then_inc(g_pe, 1)
                        pe.wait_ge(g_dv, DV2 * c + 4 + 5 * pr)
                        if pr == 0:
                            if c >= 1:
                                pe.wait_ge(g_ac, ACB + AC2 * (c - 1) + 5)
                        else:
                            pe.wait_ge(g_ac, ACB + AC2 * c + 3)
                        for j, hh in enumerate((h0, h1)):
                            pe.matmul(psL[:, 129 * j:129 * (j + 1)], retb[:, 128 * j:128 * (j + 1)],
                                      vaug[:, 129 * hh:129 * (hh + 1)], start=True, stop=True)
                            pe.matmul(psX[:, 129 * j:129 * (j + 1)],
                                      qT[:, 128 * hh:128 * (hh + 1)],
                                      stateb[hh][:, :], start=True, stop=True)
                            last = pe.matmul(psU[:, 129 * j:129 * (j + 1)],
                                             k2[:, 128 * hh:128 * (hh + 1)],
                                             vaug[:, 129 * hh:129 * (hh + 1)], start=True, stop=True)
                        last.then_inc(g_pe, 1)

            @B.scalar
            def _(ac, hg=hg, g_pe=g_pe, g_ac=g_ac, g_dv=g_dv):
                for c in range(NCH):
                    b = c % 2
                    ac.wait_ge(g_pe, PEB + PE2 * c + 3)
                    ac.activation(qnat[:, :], psA[:, 0:512], AF.Identity, scale=qscb[:, 0:1]).then_inc(g_ac, 1)
                    ac.activation(knat[:, :], psB[:, 0:512], AF.Copy).then_inc(g_ac, 1)
                    for pr in range(2):
                        h0, h1 = 2 * pr, 2 * pr + 1
                        ac.wait_ge(g_pe, PEB + PE2 * c + 7 + 2 * pr)
                        for j, hh in enumerate((h0, h1)):
                            ac.activation(crosse[:, 129 * j:129 * (j + 1)],
                                          psX[:, 129 * j:129 * (j + 1)], AF.Identity,
                                          scale=erB[:, HPG * hg + hh:HPG * hg + hh + 1])
                        ac.drain(); ac.sem_inc(g_ac, 1)
                        ac.wait_ge(g_dv, DV2 * c + 6 + 5 * pr)
                        for hh in (h0, h1):
                            ac.activation(stateb[hh][:, :], state[hh][:, :], AF.Copy)
                        ac.drain(); ac.sem_inc(g_ac, 1)
                    ac.wait_ge(g_dv, DV2 * c + 13)
                    for h in range(HPG):
                        ac.activation(gsq[:, 128 * h:128 * (h + 1)], attnc[:, 128 * h:128 * (h + 1)],
                                      AF.Square, accum_out=gst[:, 4 + h:5 + h])
                    ac.drain(); ac.sem_inc(g_ac, 1)
                    ac.wait_ge(g_dv, DV2 * c + 15)
                    ac.activation(gst[:, 20:24], gst[:, 16:20], AF.Sqrt, bias=epsb[:, 0:1])
                    ac.drain(); ac.sem_inc(g_ac, 1)
                    ac.wait_ge(g_dv, DV2 * c + 16)
                    for h in range(HPG):
                        last = ac.activation(gsq[:, 128 * h:128 * (h + 1)], attnc[:, 128 * h:128 * (h + 1)],
                                             AF.Identity, scale=gst[:, 24 + h:25 + h],
                                             bias=gst[:, 28 + h:29 + h])
                    last.then_inc(g_ac, 1)

            @B.vector
            def _(dv, hg=hg, g_pe=g_pe, g_ac=g_ac, g_dv=g_dv, g_st=g_st, cw=cw):
                for c in range(NCH):
                    b = c % 2
                    dv.wait_ge(g_pe, PEB + PE2 * c + 2)
                    for h in range(HPG):
                        last = dv.tensor_copy(vaug[:, 129 * h:129 * h + 128], psC[:, 128 * h:128 * (h + 1)])
                    last.then_inc(g_dv, 1)
                    dv.wait_ge(g_ac, ACB + AC2 * c + 2)
                    for h in range(HPG):
                        last = dv.tensor_scalar(k2[:, 128 * h:128 * (h + 1)], knat[:, 128 * h:128 * (h + 1)],
                                                ekB[:, HPG * hg + h:HPG * hg + h + 1], None, OP.mult)
                    last.then_inc(g_dv, 1)
                    dv.wait_ge(g_pe, PEB + PE2 * c + 5)
                    dv.tensor_copy(qT[:, :], psT0[:, 0:512])
                    dv.tensor_copy(kTb[:, :], psT1[:, 0:512]).then_inc(g_dv, 1)
                    for pr in range(2):
                        h0, h1 = 2 * pr, 2 * pr + 1
                        dv.wait_ge(g_pe, PEB + PE2 * c + 6 + 2 * pr)
                        dv.tensor_tensor(retb[:, 0:256], psS[:, 0:256],
                                         Dloc[:, 256 * pr:256 * (pr + 1)], OP.mult).then_inc(g_dv, 1)
                        dv.wait_ge(g_ac, ACB + AC2 * c + 3 + 2 * pr)
                        dv.tensor_add(ataug[:, 0:258], crosse[:, 0:258],
                                      psL[:, 0:258]).then_inc(g_dv, 1)
                        dv.wait_ge(g_pe, PEB + PE2 * c + 7 + 2 * pr)
                        for hh in (h0, h1):
                            j = hh - 2 * pr
                            dv.scalar_tensor_tensor(state[hh][:, :], state[hh][:, :],
                                                    edC[:, HPG * hg + hh:HPG * hg + hh + 1],
                                                    psU[:, 129 * j:129 * (j + 1)], OP.mult, OP.add)
                        dv.drain(); dv.sem_inc(g_dv, 1)
                        for j in range(2):
                            dv.tensor_scalar(rsc[:, j:j + 1], ataug[:, 129 * j + 128:129 * j + 129],
                                             -1.0, None, OP.mult)
                        nop2(dv)
                        for j in range(2):
                            dv.tensor_tensor(rsc[:, 2 + j:3 + j], rsc[:, j:j + 1],
                                             ataug[:, 129 * j + 128:129 * j + 129], OP.max)
                        nop2(dv)
                        dv.tensor_scalar(rsc[:, 4:6], rsc[:, 2:4], 1.0, None, OP.max)
                        nop2(dv)
                        dv.reciprocal(rsc[:, 6:8], rsc[:, 4:6])
                        dv.drain(); dv.sem_inc(g_dv, 1)
                        if c >= 1 and pr == 0:
                            dv.wait_ge(g_ac, ACB + AC2 * (c - 1) + 9)
                        nop2(dv)
                        for j, hh in enumerate((h0, h1)):
                            last = dv.tensor_scalar(attnc[:, 128 * hh:128 * (hh + 1)],
                                                    ataug[:, 129 * j:129 * j + 128],
                                                    rsc[:, 6 + j:7 + j], None, OP.mult)
                        last.then_inc(g_dv, 1)
                    for h in range(HPG):
                        dv.reduce_sum(gst[:, h:h + 1], attnc[:, 128 * h:128 * (h + 1)], axis=AX)
                    dv.drain(); dv.sem_inc(g_dv, 1)
                    dv.wait_ge(g_ac, ACB + AC2 * c + 7)
                    nop2(dv)
                    dv.tensor_scalar(gst[:, 8:12], gst[:, 0:4], INV_HD, None, OP.mult)
                    nop2(dv)
                    dv.tensor_tensor(gst[:, 12:16], gst[:, 8:12], gst[:, 8:12], OP.mult)
                    nop2(dv)
                    dv.scalar_tensor_tensor(gst[:, 16:20], gst[:, 4:8], INV_HD, gst[:, 12:16],
                                            OP.mult, OP.subtract)
                    dv.drain(); dv.sem_inc(g_dv, 1)
                    dv.wait_ge(g_ac, ACB + AC2 * c + 8)
                    dv.reciprocal(gst[:, 24:28], gst[:, 20:24])
                    nop2(dv)
                    dv.scalar_tensor_tensor(gst[:, 28:32], gst[:, 8:12], -1.0, gst[:, 24:28],
                                            OP.mult, OP.mult)
                    dv.drain(); dv.sem_inc(g_dv, 1)
                    dv.wait_ge(g_ac, ACB + AC2 * c + 9)
                    if c >= 1:
                        dv.wait_ge(g_st, 16 * c)
                    dv.tensor_tensor(gsq[:, :], gsq[:, :], gnwB[:, cw:cw + WG], OP.mult)
                    dv.tensor_tensor(gsq[:, :], gsq[:, :], gnbB[:, cw:cw + WG], OP.add)
                    dv.tensor_tensor(Pbuf[:, :], gsq[:, :], big[:, RG + 512 * c:RG + 512 * (c + 1)],
                                     OP.mult).then_inc(g_dv, 1)

            @B.gpsimd
            def _(pl, g_dv=g_dv, g_st=g_st, cw=cw):
                for c in range(NCH):
                    pl.wait_ge(g_dv, DV2 * c + DV2)
                    pl.dma_start(dram["Ps"].ap()[128 * c:128 * (c + 1), cw:cw + WG],
                                 Pbuf[:, :]).then_inc(g_st, 16)

            @B.sync
            def _(sp, g_st=g_st):
                sp.wait_ge(g_st, 16 * NCH)

        prev_pl = [(g_wq, 16)]

    if stop_after == "P2":
        dbg_sem = sem("dbgsem")
        with nc.Block() as B:
            @B.gpsimd
            def _(pl):
                pl.wait_ge(prev_pl, prev_pl_cnt)
            @B.sync
            def _(sp):
                sp.sem_inc(dbg_sem, 1)
                sp.wait_ge(dbg_sem, 1)
        stack2.close()
        return nc, stack

    stack2.close()

    stack3 = contextlib.ExitStack()

    def sb3(name, shape, dtype):
        return stack3.enter_context(nc.sbuf_tensor(name, shape, dtype))

    xb = [sb3(f"xc{i}", [128, E], F32) for i in range(2)]
    x1b = [sb3(f"x1c{i}", [128, E], F32) for i in range(2)]
    ub = sb3("uc", [128, E], F32)
    tbuf = sb3("tc", [128, E], BF16)
    boRb = sb3("boRb", [1, E], BF16)
    tbuf_r = tbuf.rearrange("p (k e) -> p k e", k=NK)

    q_wt = sem("q3wt")
    q_p = [sem(f"q3p{i}") for i in range(2)]
    q_x = [sem(f"q3x{i}") for i in range(2)]
    q_pl = sem("q3pl")
    q_pe = sem("q3pe")
    q_ac = sem("q3ac")
    q_dv = sem("q3dv")
    q_st = sem("q3st")
    P3, A3, V3 = 16, 6, 22
    with nc.Block() as B:
        @B.gpsimd
        def _(pl):
            for ps_, pv_ in prev_pl:
                pl.wait_ge(ps_, pv_)
            pl.dma_start(boRb[0:1, :], dram["boR"].ap()[:, :]).then_inc(q_pl, 16)
            src = dram["Wo"].ap().rearrange("(k p) n -> p k n", p=128)
            pl.dma_start(xn2T.rearrange("p (k n) -> p k n", k=NK), src).then_inc(q_pl, 16)
            for c in range(NCH):
                pl.wait_ge(q_dv, V3 * c + V3)
                pl.dma_start(dram["y"].ap()[128 * c:128 * (c + 1), :], xb[c % 2][:, :]).then_inc(q_st, 16)
            pl.wait_ge(q_st, 16 * NCH)

        @B.sync
        def _(sp):
            for ps_, pv_ in prev_pl:
                sp.wait_ge(ps_, pv_)
            sp.dma_start(big[:, :], dram["Wts"].ap()[:, :]).then_inc(q_wt, 16)
            for c in range(NCH):
                b = c % 2
                if c >= 2:
                    sp.wait_ge(q_pe, P3 * (c - 2) + 12)
                    sp.wait_ge(q_st, 16 * (c - 1))
                sp.dma_start(znb[b][:, :], dram["Ps"].ap()[128 * c:128 * (c + 1), :]).then_inc(q_p[b], 16)
                sp.dma_start(xb[b][:, :], dram["x1s"].ap()[128 * c:128 * (c + 1), :]).then_inc(q_x[b], 16)

        @B.tensor
        def _(pe):
            pe.wait_ge(q_pl, 32)
            for c in range(NCH):
                b = c % 2
                pe.wait_ge(q_p[b], 16 * (c // 2 + 1))
                for g in range(4):
                    if g >= 2:
                        pe.wait_ge(q_dv, V3 * c + (g - 2) + 1)
                    elif c > 0:
                        pe.wait_ge(q_dv, V3 * (c - 1) + 15 + (g + 2))
                    for t in range(4):
                        k = 4 * g + t
                        last = pe.transpose(psTx[g % 2][:, 128 * t:128 * (t + 1)],
                                            znb[b][:, 128 * k:128 * (k + 1)], IDENT[:, :])
                    last.then_inc(q_pe, 1)
                pe.wait_ge(q_dv, V3 * c + 4)
                for nb in range(4):
                    bank = psA if nb % 2 == 0 else psB
                    if nb >= 2:
                        pe.wait_ge(q_dv, V3 * c + 4 + (nb - 2) + 1)
                    elif c > 0:
                        pe.wait_ge(q_dv, V3 * (c - 1) + 19 + (nb + 2))
                    for k in range(NK):
                        pe.matmul(bank[:, 0:512], tbuf[:, 128 * k:128 * (k + 1)],
                                  xn2T_r[:, k, 512 * nb:512 * (nb + 1)], start=(k == 0), stop=False)
                    pe.matmul(bank[:, 0:512], onesb[0:1, 0:128], boRb[0:1, 512 * nb:512 * (nb + 1)],
                              start=False, stop=True).then_inc(q_pe, 1)
                pe.wait_ge(q_ac, A3 * c + 6)
                for g in range(4):
                    if g >= 2:
                        pe.wait_ge(q_dv, V3 * c + 15 + (g - 2))
                    else:
                        pe.wait_ge(q_dv, V3 * c + (g + 2) + 1)
                    for t in range(4):
                        k = 4 * g + t
                        last = pe.transpose(psTx[g % 2][:, 128 * t:128 * (t + 1)],
                                            znb[b][:, 128 * k:128 * (k + 1)], IDENT[:, :])
                    last.then_inc(q_pe, 1)
                pe.wait_ge(q_dv, V3 * c + 18)
                if c == 0:
                    pe.wait_ge(q_wt, 16)
                for nb in range(4):
                    bank = psA if nb % 2 == 0 else psB
                    if nb >= 2:
                        pe.wait_ge(q_dv, V3 * c + 18 + (nb - 2) + 1)
                    else:
                        pe.wait_ge(q_dv, V3 * c + 4 + (nb + 2) + 1)
                    for k in range(NK):
                        pe.matmul(bank[:, 0:512], tbuf[:, 128 * k:128 * (k + 1)],
                                  big_r[:, k, 512 * nb:512 * (nb + 1)], start=(k == 0), stop=False)
                    pe.matmul(bank[:, 0:512], onesb[0:1, 0:128],
                              lnbrowb[0:1, 512 * nb:512 * (nb + 1)],
                              start=False, stop=True).then_inc(q_pe, 1)

        @B.scalar
        def _(ac):
            for c in range(NCH):
                b = c % 2
                ac.wait_ge(q_dv, V3 * c + 8)
                ac.activation(ub[:, :], x1b[b][:, :], AF.Square,
                              accum_out=stt[b][:, 4:5])
                ac.drain(); ac.sem_inc(q_ac, 1)
                ac.wait_ge(q_dv, V3 * c + 10)
                ac.activation(stt[b][:, 12:13], stt[b][:, 10:11], AF.Sqrt, bias=epsb[:, 0:1])
                ac.drain(); ac.sem_inc(q_ac, 1)
                ac.wait_ge(q_dv, V3 * c + 11)
                ac.activation(ub[:, :], x1b[b][:, :], AF.Identity, scale=stt[b][:, 13:14],
                              bias=stt[b][:, 14:15]).then_inc(q_ac, 1)
                ac.wait_ge(q_pe, P3 * c + 4)
                ac.activation(znb[b][:, :], ub[:, :], AF.Square,
                              accum_out=stt[b][:, 20:21])
                ac.drain(); ac.sem_inc(q_ac, 1)
                ac.wait_ge(q_dv, V3 * c + 13)
                ac.activation(stt[b][:, 28:29], stt[b][:, 26:27], AF.Sqrt, bias=epsb[:, 0:1])
                ac.drain(); ac.sem_inc(q_ac, 1)
                ac.wait_ge(q_dv, V3 * c + 14)
                ac.activation(znb[b][:, :], ub[:, :], AF.Identity, scale=stt[b][:, 29:30],
                              bias=stt[b][:, 30:31]).then_inc(q_ac, 1)

        @B.vector
        def _(dv):
            for c in range(NCH):
                b = c % 2
                for g in range(4):
                    dv.wait_ge(q_pe, P3 * c + g + 1)
                    dv.tensor_copy(tbuf_r[:, 4 * g:4 * g + 4, :],
                                   psT_r[g % 2][:, 0:4, :]).then_inc(q_dv, 1)
                for nb in range(4):
                    dv.wait_ge(q_pe, P3 * c + 4 + nb + 1)
                    if nb == 0:
                        dv.wait_ge(q_x[b], 16 * (c // 2 + 1))
                    bank = psA if nb % 2 == 0 else psB
                    dv.tensor_add(x1b[b][:, 512 * nb:512 * (nb + 1)],
                                  xb[b][:, 512 * nb:512 * (nb + 1)],
                                  bank[:, 0:512]).then_inc(q_dv, 1)
                dv.reduce_sum(stt[b][:, 0:1], x1b[b][:, :], axis=AX)
                dv.drain(); dv.sem_inc(q_dv, 1)
                dv.wait_ge(q_ac, A3 * c + 1)
                stats3(dv, b, 0); dv.drain(); dv.sem_inc(q_dv, 1)
                dv.wait_ge(q_ac, A3 * c + 2)
                statsfin(dv, b, 0); dv.drain(); dv.sem_inc(q_dv, 1)
                dv.wait_ge(q_ac, A3 * c + 3)
                dv.reduce_sum(stt[b][:, 16:17], ub[:, :], axis=AX)
                dv.drain(); dv.sem_inc(q_dv, 1)
                dv.wait_ge(q_ac, A3 * c + 4)
                stats3(dv, b, 16); dv.drain(); dv.sem_inc(q_dv, 1)
                dv.wait_ge(q_ac, A3 * c + 5)
                statsfin(dv, b, 16); dv.drain(); dv.sem_inc(q_dv, 1)
                for g in range(4):
                    dv.wait_ge(q_pe, P3 * c + 8 + g + 1)
                    dv.tensor_copy(tbuf_r[:, 4 * g:4 * g + 4, :],
                                   psT_r[g % 2][:, 0:4, :]).then_inc(q_dv, 1)
                for nb in range(4):
                    dv.wait_ge(q_pe, P3 * c + 12 + nb + 1)
                    if nb == 0 and c >= 2:
                        dv.wait_ge(q_st, 16 * (c - 1))
                    bank = psA if nb % 2 == 0 else psB
                    dv.tensor_add(xb[b][:, 512 * nb:512 * (nb + 1)],
                                  x1b[b][:, 512 * nb:512 * (nb + 1)],
                                  bank[:, 0:512]).then_inc(q_dv, 1)

    stack3.close()
    return nc, stack


def _host_inputs(inputs, core):
    f = np.ascontiguousarray
    d = {
        "x": f(np.asarray(inputs["L0_attn_x"])[core]),
        "Wf": f(np.asarray(inputs["L0_ffn_W"])),
        "Wq": f(np.asarray(inputs["L0_attn_Wq"])),
        "Wk": f(np.asarray(inputs["L0_attn_Wk"])),
        "Wv": f(np.asarray(inputs["L0_attn_Wv"])),
        "Wg": f(np.asarray(inputs["L0_attn_Wg"])),
        "Wo": f(np.asarray(inputs["L0_attn_Wo"])),
        "lnwT": f(np.asarray(inputs["L0_ffn_ln_w"]).reshape(NK, 128).T),
        "lnbT": f(np.asarray(inputs["L0_ffn_ln_b"]).reshape(NK, 128).T),
        "bqR": f(np.asarray(inputs["L0_attn_bq"]).reshape(1, E)),
        "bkR": f(np.asarray(inputs["L0_attn_bk"]).reshape(1, E)),
        "bvR": f(np.asarray(inputs["L0_attn_bv"]).reshape(1, E)),
        "bgR": f(np.asarray(inputs["L0_attn_bg"]).reshape(1, E)),
        "boR": f(np.asarray(inputs["L0_attn_bo"]).reshape(1, E)),
        "gnwB": f(np.broadcast_to(np.asarray(inputs["L0_attn_gn_w"]), (128, E))),
        "gnbB": f(np.broadcast_to(np.asarray(inputs["L0_attn_gn_b"]), (128, E))),
        "decB": f(np.broadcast_to(np.asarray(inputs["L0_attn_decay"]), (128, H))),
    }
    ii = np.arange(128, dtype=np.float32)
    diff = ii[None, :] - ii[:, None]
    d["DIFFT"] = f(np.where(diff >= 0, diff, BIGNEG).astype(np.float32))
    d["ROWP1"] = f((ii + 1.0).reshape(128, 1))
    d["R127M"] = f((127.0 - ii).reshape(128, 1))
    d["IDENT"] = f(np.eye(128, dtype=ml_dtypes.bfloat16))
    return d


def kernel(**inputs):
    nc, stack = build(debug=False)
    in_maps = [_host_inputs(inputs, core) for core in range(8)]
    res = run_bass_kernel_spmd(nc, in_maps, core_ids=list(range(8)))
    out = np.stack([np.asarray(res.results[i]["y"]) for i in range(8)], axis=0)
    return np.ascontiguousarray(out.astype(np.float32, copy=False))


# version 06
# speedup vs baseline: 1.0272x; 1.0059x over previous
import contextlib
import numpy as np
import ml_dtypes
import concourse.bass as bass
import concourse.mybir as mybir
from concourse.bass_utils import run_bass_kernel_spmd

F32 = mybir.dt.float32
BF16 = mybir.dt.bfloat16
AF = mybir.ActivationFunctionType
OP = mybir.AluOpType
AX = mybir.AxisListType.X

S = 2048
E = 2048
H = 16
HD = 128
NCH = 16
NK = 16
NHG = 4
HPG = 4
WG = 512
EPS = 1e-5
QSCALE = float(HD) ** -0.5
INV_E = 1.0 / float(E)
INV_HD = 1.0 / float(HD)
BIGNEG = 1.0e9


def build(debug=False, stop_after=None, nch1=NCH, skip=()):
    nc = bass.Bass("TRN2", target_bir_lowering=False)
    dram = {}

    def dt(name, shape, dtype, kind):
        dram[name] = nc.dram_tensor(name, shape, dtype, kind=kind)

    dt("x", [S, E], F32, "ExternalInput")
    for nm in ["Wf", "Wq", "Wk", "Wv", "Wg", "Wo"]:
        dt(nm, [E, E], F32, "ExternalInput")
    dt("lnwT", [128, NK], F32, "ExternalInput")
    dt("lnbT", [128, NK], F32, "ExternalInput")
    for nm in ["bqR", "bkR", "bvR", "bgR", "boR"]:
        dt(nm, [1, E], F32, "ExternalInput")
    dt("gnwB", [128, E], F32, "ExternalInput")
    dt("gnbB", [128, E], F32, "ExternalInput")
    dt("decB", [128, H], F32, "ExternalInput")
    dt("DIFFT", [128, 128], F32, "ExternalInput")
    dt("ROWP1", [128, 1], F32, "ExternalInput")
    dt("R127M", [128, 1], F32, "ExternalInput")
    dt("IDENT", [128, 128], BF16, "ExternalInput")
    dt("y", [S, E], F32, "ExternalOutput")
    sk = "ExternalOutput" if debug else "Internal"
    dt("x1s", [S, E], F32, sk)
    dt("Ps", [S, E], BF16, sk)
    dt("Wts", [128, NK * E], BF16, sk)

    stack = contextlib.ExitStack()

    def sb(name, shape, dtype):
        return stack.enter_context(nc.sbuf_tensor(name, shape, dtype))

    def psum(name, shape, dtype):
        return stack.enter_context(nc.psum_tensor(name, shape, dtype))

    def sem(name):
        return stack.enter_context(nc.semaphore(name))

    xn2T = sb("xn2T", [128, NK * E], BF16)
    big = sb("big", [128, NK * E], BF16)
    znb = [sb(f"znb{i}", [128, E], BF16) for i in range(2)]
    gnwB = sb("gnwBs", [128, E], F32)
    gnbB = sb("gnbBs", [128, E], F32)
    lnwT = sb("lnwTs", [128, NK], F32)
    lnbT = sb("lnbTs", [128, NK], F32)
    decB = sb("decBs", [128, H], F32)
    DIFFT = sb("DIFFTs", [128, 128], F32)
    ROWP1 = sb("ROWP1s", [128, 1], F32)
    R127M = sb("R127Ms", [128, 1], F32)
    IDENT = sb("IDENTs", [128, 128], BF16)
    onesb = sb("onesb", [1, 128], BF16)
    onesf = sb("onesf", [128, 1], F32)
    onesrf = sb("onesrf", [1, 128], F32)
    lnbrowb = sb("lnbrowb", [1, E], BF16)
    halflnw = sb("halflnw", [128, NK], F32)
    lnbTb = sb("lnbTb", [128, NK], BF16)
    erB = sb("erB", [128, H], F32)
    ekB = sb("ekB", [128, H], F32)
    edC = sb("edC", [128, H], F32)
    stt = [sb(f"stt{i}", [128, 48], F32) for i in range(2)]
    wsc = sb("wsc", [128, 1], F32)
    scl1 = sb("scl1", [1, 1], F32)
    epsb = sb("epsb", [128, 1], F32)
    qscb = sb("qscb", [128, 1], F32)
    c128b = sb("c128b", [128, 1], F32)
    psA = psum("psA", [128, 512], F32)
    psB = psum("psB", [128, 512], F32)
    psC = psum("psC", [128, 512], F32)
    psS = psum("psS", [128, 512], F32)
    psL = psum("psL", [128, 512], F32)
    psX = psum("psX", [128, 512], F32)
    psT0 = psum("psT0", [128, 1024], BF16)
    psT1 = psum("psT1", [128, 1024], BF16)
    psTx = [psT0, psT1]
    psU = psC

    xn2T_r = xn2T.rearrange("p (k e) -> p k e", k=NK)
    big_r = big.rearrange("p (k e) -> p k e", k=NK)
    psT_r = [t.rearrange("p (k e) -> p k e", k=8) for t in psTx]

    def nop2(dv):
        dv.drain()

    def stats3(dv, b, base):
        nop2(dv)
        dv.tensor_scalar(stt[b][:, base + 8:base + 9], stt[b][:, base + 0:base + 1], INV_E, None, OP.mult)
        nop2(dv)
        dv.tensor_tensor(stt[b][:, base + 9:base + 10], stt[b][:, base + 8:base + 9],
                         stt[b][:, base + 8:base + 9], OP.mult)
        nop2(dv)
        return dv.scalar_tensor_tensor(stt[b][:, base + 10:base + 11], stt[b][:, base + 4:base + 5], INV_E,
                                       stt[b][:, base + 9:base + 10], OP.mult, OP.subtract)

    def statsfin(dv, b, base):
        dv.reciprocal(stt[b][:, base + 13:base + 14], stt[b][:, base + 12:base + 13])
        nop2(dv)
        return dv.scalar_tensor_tensor(stt[b][:, base + 14:base + 15], stt[b][:, base + 8:base + 9], -1.0,
                                       stt[b][:, base + 13:base + 14], OP.mult, OP.mult)

    stack1 = contextlib.ExitStack()

    def sb1(name, shape, dtype):
        return stack1.enter_context(nc.sbuf_tensor(name, shape, dtype))

    xb = [sb1(f"xb{i}", [128, E], F32) for i in range(2)]
    x1b = [sb1(f"x1b{i}", [128, E], F32) for i in range(2)]
    ub = sb1("ub", [128, E], F32)
    tbuf = sb1("tbuf", [128, E], BF16)
    ternb = sb1("ternb", [128, E], BF16)
    tbuf_r = tbuf.rearrange("p (k e) -> p k e", k=NK)

    s0_ld = sem("s0ld")
    s0_w = [sem(f"s0w{i}") for i in range(4)]
    s0_ac = sem("s0ac")
    s0_dv = sem("s0dv")
    s0_pe = sem("s0pe")

    with nc.Block() as B:
        @B.sync
        def _(sp):
            small = [("lnwT", lnwT), ("lnbT", lnbT), ("gnwB", gnwB), ("gnbB", gnbB),
                     ("decB", decB), ("DIFFT", DIFFT), ("ROWP1", ROWP1),
                     ("R127M", R127M), ("IDENT", IDENT)]
            for nm, buf in small:
                sp.dma_start(buf[:, :], dram[nm].ap()[:, :]).then_inc(s0_ld, 16)
            for k in range(NK):
                if k >= 4:
                    sp.wait_ge(s0_ac, k - 3)
                buf = [xb[0], xb[1], x1b[0], x1b[1]][k % 4]
                sp.dma_start(buf[:, :], dram["Wf"].ap()[128 * k:128 * (k + 1), :]).then_inc(s0_w[k % 4], 16)
            sp.wait_ge(s0_ld, 16 * 9)

        @B.scalar
        def _(ac):
            for k in range(NK):
                ac.wait_ge(s0_w[k % 4], 16 * (k // 4 + 1))
                buf = [xb[0], xb[1], x1b[0], x1b[1]][k % 4]
                ac.activation(ub[:, :], buf[:, :], AF.Abs,
                              accum_out=stt[0][:, k:k + 1])
                ac.drain(); ac.sem_inc(s0_ac, 1)

        @B.vector
        def _(dv):
            dv.memset(onesb[:, :], 1.0)
            dv.memset(onesf[:, :], 1.0)
            dv.memset(onesrf[:, :], 1.0)
            dv.memset(epsb[:, :], EPS)
            dv.memset(qscb[:, :], QSCALE)
            dv.memset(c128b[:, :], 128.0)
            dv.wait_ge(s0_ac, NK)
            dv.reduce_sum(wsc[:, 0:1], stt[0][:, 0:NK], axis=AX)
            dv.drain(); dv.sem_inc(s0_dv, 1)

        @B.tensor
        def _(pe):
            pe.wait_ge(s0_dv, 1)
            pe.matmul(psA[0:1, 0:1], wsc[:, 0:1], onesf[:, 0:1],
                      start=True, stop=True).then_inc(s0_pe, 1)

        @B.gpsimd
        def _(pl):
            pass

    with nc.Block() as B:
        @B.vector
        def _(dv):
            dv.tensor_scalar(scl1[0:1, 0:1], psA[0:1, 0:1], 0.5 * INV_E * INV_E, None, OP.mult)

    s0_bc = sem("s0bc")
    with nc.Block() as B:
        @B.vector
        def _(dv):
            dv.tensor_scalar(onesrf[0:1, 0:128], onesrf[0:1, 0:128], scl1[0:1, 0:1], None, OP.mult)
            dv.tensor_scalar(halflnw[:, :], lnwT[:, :], 0.5, None, OP.mult)
            dv.tensor_copy(lnbTb[:, :], lnbT[:, :])

        @B.sync
        def _(sp):
            pass
    with nc.Block() as B:
        @B.sync
        def _(sp):
            sp.dma_start(dram["x1s"].ap()[0:1, 0:128], onesrf[0:1, 0:128]).then_inc(s0_bc, 16)
            sp.wait_ge(s0_bc, 16)
            src = dram["x1s"].ap()[0:1, 0:128].rearrange("a b -> b a")
            sp.dma_start(wsc[:, 0:1], src).then_inc(s0_bc, 16)
            sp.wait_ge(s0_bc, 32)
        @B.vector
        def _(dv):
            dv.wait_ge(s0_bc, 32)
            dv.memset(onesrf[:, :], 1.0)

        @B.scalar
        def _(ac):
            for h in range(H):
                ac.activation(erB[:, h:h + 1], ROWP1[:, 0:1], AF.Exp, scale=decB[:, h:h + 1])
                ac.activation(ekB[:, h:h + 1], R127M[:, 0:1], AF.Exp, scale=decB[:, h:h + 1])
            ac.activation(edC[:, :], decB[:, :], AF.Exp, scale=c128b[:, 0:1])

    s2_ld = [sem(f"s2ld{i}") for i in range(2)]
    s2_ac = sem("s2ac")
    s2_dv = sem("s2dv")
    s2_pe = sem("s2pe")
    with nc.Block() as B:
        @B.sync
        def _(sp):
            for k in range(NK):
                if k >= 2:
                    sp.wait_ge(s2_ac, 2 * (k - 1))
                sp.dma_start(xb[k % 2][:, :], dram["Wf"].ap()[128 * k:128 * (k + 1), :]).then_inc(s2_ld[k % 2], 16)

        @B.scalar
        def _(ac):
            for k in range(NK):
                ac.wait_ge(s2_ld[k % 2], 16 * (k // 2 + 1))
                if k >= 2:
                    ac.wait_ge(s2_pe, k - 1)
                    ac.wait_ge(s2_dv, 2 * (k - 1))
                ac.activation(znb[k % 2][:, :], xb[k % 2][:, :], AF.Sign).then_inc(s2_ac, 1)
                ac.activation(x1b[k % 2][:, :], xb[k % 2][:, :], AF.Abs).then_inc(s2_ac, 1)

        @B.vector
        def _(dv):
            for k in range(NK):
                dv.wait_ge(s2_ac, 2 * (k + 1))
                if k >= 1:
                    dv.wait_ge(s2_pe, k)
                    dv.wait_ge(s2_dv, 2 * k)
                dv.scalar_tensor_tensor(ternb[:, :], x1b[k % 2][:, :], wsc[:, 0:1],
                                        znb[k % 2][:, :], OP.is_ge, OP.mult).then_inc(s2_dv, 1)
                dv.wait_ge(s2_dv, 2 * k + 1)
                dv.tensor_scalar(big_r[:, k, :], ternb[:, :],
                                 halflnw[:, k:k + 1], None, OP.mult).then_inc(s2_dv, 1)

        @B.tensor
        def _(pe):
            pbank = [psA, psB, psC, psS]
            for k in range(NK):
                pe.wait_ge(s2_dv, 2 * k + 1)
                last = None
                for nb in range(4):
                    last = pe.matmul(pbank[nb][0:1, 0:512], lnbTb[:, k:k + 1],
                                     ternb[:, 512 * nb:512 * (nb + 1)],
                                     start=(k == 0), stop=(k == NK - 1))
                last.then_inc(s2_pe, 1)

    with nc.Block() as B:
        @B.vector
        def _(dv):
            pbank = [psA, psB, psC, psS]
            for nb in range(4):
                dv.tensor_scalar(lnbrowb[0:1, 512 * nb:512 * (nb + 1)], pbank[nb][0:1, 0:512],
                                 0.5, None, OP.mult)

    if stop_after == "P0":
        dbg_sem = sem("dbgsem")
        with nc.Block() as B:
            @B.sync
            def _(sp):
                sp.dma_start(dram["Wts"].ap()[:, :], big[:, :]).then_inc(dbg_sem, 16)
                with nc.allow_non_contiguous_dma(reason="debug"):
                    sp.dma_start(dram["x1s"].ap()[0:128, 0:1], wsc[:, 0:1]).then_inc(dbg_sem, 16)
                    sp.dma_start(dram["x1s"].ap()[0:128, 8:32], stt[0][:, :]).then_inc(dbg_sem, 16)
                    sp.dma_start(dram["x1s"].ap()[0:1, 40:41], scl1[0:1, 0:1]).then_inc(dbg_sem, 16)
                    sp.dma_start(dram["x1s"].ap()[0:128, 50:51], onesf[:, 0:1]).then_inc(dbg_sem, 16)
                sp.wait_ge(dbg_sem, 80)
        stack1.close()
        return nc, stack

    p1_ld = [sem(f"p1ld{i}") for i in range(2)]
    p1_sp = sem("p1sp")
    p1_st = [sem(f"p1st{i}") for i in range(2)]
    p1_ac = sem("p1ac")
    p1_dv = sem("p1dv")
    p1_pe = sem("p1pe")
    A1, V1, P1 = 9, 21, 12

    with nc.Block() as B:
        @B.sync
        def _(sp):
            for c in range(nch1):
                if c >= 2:
                    sp.wait_ge(p1_dv, V1 * (c - 2) + 14)
                sp.dma_start(xb[c % 2][:, :], dram["x"].ap()[128 * c:128 * (c + 1), :]).then_inc(p1_ld[c % 2], 16)

        @B.gpsimd
        def _(pl):
            if "wts" in skip:
                pl.sem_inc(p1_sp, 16)
            else:
                pl.dma_start(dram["Wts"].ap()[:, :], big[:, :]).then_inc(p1_sp, 16)
            for c in range(nch1):
                pl.wait_ge(p1_dv, V1 * c + 14)
                if "stores" in skip:
                    pl.sem_inc(p1_st[c % 2], 16)
                else:
                    pl.dma_start(dram["x1s"].ap()[128 * c:128 * (c + 1), :],
                                 x1b[c % 2][:, :]).then_inc(p1_st[c % 2], 16)

        @B.scalar
        def _(ac):
            for c in range(nch1):
                b = c % 2
                ac.wait_ge(p1_ld[c % 2], 16 * (c // 2 + 1))
                if c >= 2:
                    ac.wait_ge(p1_st[c % 2], 16 * (c // 2))
                    ac.wait_ge(p1_dv, V1 * (c - 2) + V1)
                ac.activation(x1b[b][:, :], xb[b][:, :], AF.Square,
                              accum_out=stt[b][:, 4:5])
                ac.drain(); ac.sem_inc(p1_ac, 1)
                ac.wait_ge(p1_dv, V1 * c + 2)
                if "sqrt" in skip:
                    ac.activation(stt[b][:, 12:13], stt[b][:, 10:11], AF.Sqrt).then_inc(p1_ac, 1)
                else:
                    ac.activation(stt[b][:, 12:13], stt[b][:, 10:11], AF.Sqrt, bias=epsb[:, 0:1])
                ac.drain(); ac.sem_inc(p1_ac, 1)
                ac.wait_ge(p1_dv, V1 * c + 3)
                ac.activation(ub[:, :], xb[b][:, :], AF.Identity, scale=stt[b][:, 13:14],
                              bias=stt[b][:, 14:15]).then_inc(p1_ac, 1)
                ac.activation(x1b[b][:, :], ub[:, :], AF.Square,
                              accum_out=stt[b][:, 20:21])
                ac.drain(); ac.sem_inc(p1_ac, 1)
                ac.wait_ge(p1_dv, V1 * c + 5)
                ac.activation(stt[b][:, 28:29], stt[b][:, 26:27], AF.Sqrt, bias=epsb[:, 0:1])
                ac.drain(); ac.sem_inc(p1_ac, 1)
                ac.wait_ge(p1_dv, V1 * c + 6)
                ac.activation(znb[b][:, :], ub[:, :], AF.Identity, scale=stt[b][:, 29:30],
                              bias=stt[b][:, 30:31]).then_inc(p1_ac, 1)
                ac.wait_ge(p1_dv, V1 * c + 14)
                ac.activation(ub[:, :], x1b[b][:, :], AF.Square,
                              accum_out=stt[b][:, 36:37])
                ac.drain(); ac.sem_inc(p1_ac, 1)
                ac.wait_ge(p1_dv, V1 * c + 16)
                ac.activation(stt[b][:, 44:45], stt[b][:, 42:43], AF.Sqrt, bias=epsb[:, 0:1])
                ac.drain(); ac.sem_inc(p1_ac, 1)
                ac.wait_ge(p1_dv, V1 * c + 17)
                ac.activation(znb[b][:, :], x1b[b][:, :], AF.Identity, scale=stt[b][:, 45:46],
                              bias=stt[b][:, 46:47]).then_inc(p1_ac, 1)

        @B.vector
        def _(dv):
            for c in range(nch1):
                b = c % 2
                dv.wait_ge(p1_ld[c % 2], 16 * (c // 2 + 1))
                if c >= 2:
                    dv.wait_ge(p1_ac, A1 * (c - 2) + A1)
                dv.reduce_sum(stt[b][:, 0:1], xb[b][:, :], axis=AX)
                dv.drain(); dv.sem_inc(p1_dv, 1)
                dv.wait_ge(p1_ac, A1 * c + 1)
                stats3(dv, b, 0); dv.drain(); dv.sem_inc(p1_dv, 1)
                dv.wait_ge(p1_ac, A1 * c + 2)
                statsfin(dv, b, 0); dv.drain(); dv.sem_inc(p1_dv, 1)
                dv.wait_ge(p1_ac, A1 * c + 3)
                dv.reduce_sum(stt[b][:, 16:17], ub[:, :], axis=AX)
                dv.drain(); dv.sem_inc(p1_dv, 1)
                dv.wait_ge(p1_ac, A1 * c + 4)
                stats3(dv, b, 16); dv.drain(); dv.sem_inc(p1_dv, 1)
                dv.wait_ge(p1_ac, A1 * c + 5)
                statsfin(dv, b, 16); dv.drain(); dv.sem_inc(p1_dv, 1)
                for g in range(4):
                    dv.wait_ge(p1_pe, P1 * c + g + 1)
                    if "t1" not in skip:
                        dv.tensor_copy(tbuf_r[:, 4 * g:4 * g + 4, :],
                                       psT_r[g % 2][:, 0:4, :]).then_inc(p1_dv, 1)
                    else:
                        dv.sem_inc(p1_dv, 1)
                for nb in range(4):
                    dv.wait_ge(p1_pe, P1 * c + 4 + nb + 1)
                    bank = psA if nb % 2 == 0 else psB
                    if "ffn" in skip:
                        dv.tensor_copy(x1b[b][:, 512 * nb:512 * (nb + 1)],
                                       xb[b][:, 512 * nb:512 * (nb + 1)]).then_inc(p1_dv, 1)
                    else:
                        dv.tensor_add(x1b[b][:, 512 * nb:512 * (nb + 1)],
                                      xb[b][:, 512 * nb:512 * (nb + 1)],
                                      bank[:, 0:512]).then_inc(p1_dv, 1)
                dv.reduce_sum(stt[b][:, 32:33], x1b[b][:, :], axis=AX)
                dv.drain(); dv.sem_inc(p1_dv, 1)
                dv.wait_ge(p1_ac, A1 * c + 7)
                stats3(dv, b, 32); dv.drain(); dv.sem_inc(p1_dv, 1)
                dv.wait_ge(p1_ac, A1 * c + 8)
                statsfin(dv, b, 32); dv.drain(); dv.sem_inc(p1_dv, 1)
                for g in range(4):
                    dv.wait_ge(p1_pe, P1 * c + 8 + g + 1)
                    if "t2" not in skip:
                        dv.tensor_copy(xn2T_r[:, 4 * g:4 * g + 4, 128 * c:128 * (c + 1)],
                                       psT_r[g % 2][:, 0:4, :]).then_inc(p1_dv, 1)
                    else:
                        dv.sem_inc(p1_dv, 1)

        @B.tensor
        def _(pe):
            for c in range(nch1):
                b = c % 2
                pe.wait_ge(p1_ac, A1 * c + 6)
                for g in range(4):
                    if g >= 2:
                        pe.wait_ge(p1_dv, V1 * c + 6 + (g - 2) + 1)
                    elif c > 0:
                        pe.wait_ge(p1_dv, V1 * (c - 1) + 17 + (g + 2) + 1)
                    if "t1" not in skip:
                        for t in range(4):
                            k = 4 * g + t
                            last = pe.transpose(psTx[g % 2][:, 128 * t:128 * (t + 1)],
                                                znb[b][:, 128 * k:128 * (k + 1)], IDENT[:, :])
                        last.then_inc(p1_pe, 1)
                    else:
                        pe.sem_inc(p1_pe, 1)
                pe.wait_ge(p1_dv, V1 * c + 10)
                for nb in range(4):
                    bank = psA if nb % 2 == 0 else psB
                    if nb >= 2:
                        pe.wait_ge(p1_dv, V1 * c + 10 + (nb - 2) + 1)
                    elif c > 0:
                        pe.wait_ge(p1_dv, V1 * (c - 1) + 10 + (nb + 2) + 1)
                    if "ffn" not in skip:
                        for k in range(NK):
                            pe.matmul(bank[:, 0:512], tbuf[:, 128 * k:128 * (k + 1)],
                                      big_r[:, k, 512 * nb:512 * (nb + 1)], start=(k == 0), stop=False)
                        pe.matmul(bank[:, 0:512], onesb[0:1, 0:128],
                                  lnbrowb[0:1, 512 * nb:512 * (nb + 1)],
                                  start=False, stop=True).then_inc(p1_pe, 1)
                    else:
                        pe.sem_inc(p1_pe, 1)
                pe.wait_ge(p1_ac, A1 * c + 9)
                for g in range(4):
                    if g >= 2:
                        pe.wait_ge(p1_dv, V1 * c + 17 + (g - 2) + 1)
                    else:
                        pe.wait_ge(p1_dv, V1 * c + 6 + (g + 2) + 1)
                    if "t2" not in skip:
                        for t in range(4):
                            k = 4 * g + t
                            last = pe.transpose(psTx[g % 2][:, 128 * t:128 * (t + 1)],
                                                znb[b][:, 128 * k:128 * (k + 1)], IDENT[:, :])
                        last.then_inc(p1_pe, 1)
                    else:
                        pe.sem_inc(p1_pe, 1)

    if stop_after == "P1":
        dbg_sem = sem("dbgsem")
        with nc.Block() as B:
            @B.gpsimd
            def _(pl):
                pl.wait_ge(p1_sp, 16)
                for i in range(2):
                    pl.wait_ge(p1_st[i], 16 * (nch1 // 2))
            @B.sync
            def _(sp):
                if "dump" in skip:
                    sp.sem_inc(dbg_sem, 16)
                else:
                    sp.dma_start(dram["Ps"].ap().rearrange("(a p) n -> p a n", p=128),
                                 xn2T.rearrange("p (a n) -> p a n", a=16)).then_inc(dbg_sem, 16)
                sp.wait_ge(dbg_sem, 16)
        stack1.close()
        return nc, stack

    stack1.close()

    stack2 = contextlib.ExitStack()

    def sb2(name, shape, dtype):
        return stack2.enter_context(nc.sbuf_tensor(name, shape, dtype))

    qT = sb2("qT", [128, HPG * 128], BF16)
    kTb = sb2("kTb", [128, HPG * 128], BF16)
    qnat = sb2("qnat", [128, WG], BF16)
    knat = sb2("knat", [128, WG], BF16)
    k2 = sb2("k2", [128, HPG * 128], BF16)
    vaug = sb2("vaug", [128, HPG * 129], BF16)
    retb = sb2("retb", [128, 256], BF16)
    crosse = sb2("crosse", [128, 258], F32)
    ataug = sb2("ataug", [128, 258], F32)
    rsc = sb2("rsc", [128, 8], F32)
    state = [sb2(f"state{h}", [128, 129], F32) for h in range(HPG)]
    stateb = [sb2(f"stateb{h}", [128, 129], BF16) for h in range(HPG)]
    attnc = sb2("attnc", [128, WG], F32)
    gst = sb2("gst", [128, 32], F32)
    gsq = sb2("gsq", [128, WG], F32)
    Pbuf = sb2("Pbuf", [128, WG], BF16)
    Dloc = sb2("Dloc", [128, HPG * 128], F32)
    hgbias = sb2("hgbias", [1, 4 * WG], BF16)

    RQ, RK, RV, RG = 0, 8192, 16384, 24576
    PEB = ACB = 16
    PE2, AC2, DV2 = 9, 9, 17
    prev_pl = [(p1_sp, 16), (p1_st[0], 16 * (NCH // 2)), (p1_st[1], 16 * (NCH // 2))]
    for hg in range(NHG):
        g_bias = sem(f"g{hg}bias")
        g_wg = sem(f"g{hg}wg")
        g_wk = sem(f"g{hg}wk")
        g_wv = sem(f"g{hg}wv")
        g_wq = sem(f"g{hg}wq")
        g_pe = sem(f"g{hg}pe")
        g_ac = sem(f"g{hg}ac")
        g_dv = sem(f"g{hg}dv")
        g_st = sem(f"g{hg}st")
        cw = 512 * hg

        def wslice_dma(pl, wname, region, wsem):
            src = dram[wname].ap()[:, cw:cw + WG].rearrange("(k p) n -> p k n", p=128)
            dst = big[:, region:region + 8192].rearrange("p (k n) -> p k n", k=NK)
            pl.dma_start(dst, src).then_inc(wsem, 16)

        with nc.Block() as B:
            @B.gpsimd
            def _(pl, hg=hg, g_pe=g_pe, cw=cw, prev_pl=prev_pl,
                  g_bias=g_bias, g_wg=g_wg, g_wk=g_wk, g_wv=g_wv, g_wq=g_wq):
                if prev_pl is not None:
                    for ps_, pv_ in prev_pl:
                        pl.wait_ge(ps_, pv_)
                for i, nm in enumerate(["bgR", "bkR", "bvR", "bqR"]):
                    pl.dma_start(hgbias[0:1, WG * i:WG * (i + 1)],
                                 dram[nm].ap()[0:1, cw:cw + WG]).then_inc(g_bias, 16)
                wslice_dma(pl, "Wg", RQ, g_wg)
                wslice_dma(pl, "Wk", RK, g_wk)
                wslice_dma(pl, "Wv", RV, g_wv)
                pl.wait_ge(g_pe, NCH)
                wslice_dma(pl, "Wq", RQ, g_wq)

            @B.tensor
            def _(pe, g_pe=g_pe, g_ac=g_ac, cw=cw, g_bias=g_bias, g_wg=g_wg):
                pe.wait_ge(g_bias, 64)
                pe.wait_ge(g_wg, 16)
                for c in range(NCH):
                    bank = psA if c % 2 == 0 else psB
                    if c >= 2:
                        pe.wait_ge(g_ac, c - 1)
                    for k in range(NK):
                        pe.matmul(bank[:, 0:512], xn2T_r[:, k, 128 * c:128 * (c + 1)],
                                  big[:, RQ + 512 * k:RQ + 512 * (k + 1)], start=(k == 0), stop=False)
                    pe.matmul(bank[:, 0:512], onesb[0:1, 0:128], hgbias[0:1, 0:WG],
                              start=False, stop=True).then_inc(g_pe, 1)

            @B.scalar
            def _(ac, hg=hg, g_pe=g_pe, g_ac=g_ac):
                for h in range(HPG):
                    ac.activation(Dloc[:, 128 * h:128 * (h + 1)], DIFFT[:, :], AF.Exp,
                                  scale=decB[:, HPG * hg + h:HPG * hg + h + 1])
                for c in range(NCH):
                    bank = psA if c % 2 == 0 else psB
                    ac.wait_ge(g_pe, c + 1)
                    ac.activation(big[:, RG + 512 * c:RG + 512 * (c + 1)], bank[:, 0:512],
                                  AF.Silu).then_inc(g_ac, 1)

            @B.vector
            def _(dv):
                dv.memset(vaug[:, :], 1.0)
                for h in range(HPG):
                    dv.memset(state[h][:, :], 0.0)
                    dv.memset(stateb[h][:, :], 0.0)

        with nc.Block() as B:
            @B.tensor
            def _(pe, hg=hg, g_pe=g_pe, g_ac=g_ac, g_dv=g_dv, cw=cw,
                  g_wk=g_wk, g_wv=g_wv, g_wq=g_wq):
                for c in range(NCH):
                    b = c % 2
                    if c == 0:
                        pe.wait_ge(g_wk, 16)
                    else:
                        pe.wait_ge(g_ac, ACB + AC2 * (c - 1) + 2)
                    for k in range(NK):
                        pe.matmul(psB[:, 0:512], xn2T_r[:, k, 128 * c:128 * (c + 1)],
                                  big[:, RK + 512 * k:RK + 512 * (k + 1)], start=(k == 0), stop=False)
                    pe.matmul(psB[:, 0:512], onesb[0:1, 0:128], hgbias[0:1, WG:2 * WG],
                              start=False, stop=True).then_inc(g_pe, 1)
                    if c == 0:
                        pe.wait_ge(g_wv, 16)
                    else:
                        pe.wait_ge(g_dv, DV2 * (c - 1) + 11)
                    for k in range(NK):
                        pe.matmul(psC[:, 0:512], xn2T_r[:, k, 128 * c:128 * (c + 1)],
                                  big[:, RV + 512 * k:RV + 512 * (k + 1)], start=(k == 0), stop=False)
                    pe.matmul(psC[:, 0:512], onesb[0:1, 0:128], hgbias[0:1, 2 * WG:3 * WG],
                              start=False, stop=True).then_inc(g_pe, 1)
                    if c == 0:
                        pe.wait_ge(g_wq, 16)
                    else:
                        pe.wait_ge(g_ac, ACB + AC2 * (c - 1) + 1)
                    for k in range(NK):
                        pe.matmul(psA[:, 0:512], xn2T_r[:, k, 128 * c:128 * (c + 1)],
                                  big[:, RQ + 512 * k:RQ + 512 * (k + 1)], start=(k == 0), stop=False)
                    pe.matmul(psA[:, 0:512], onesb[0:1, 0:128], hgbias[0:1, 3 * WG:4 * WG],
                              start=False, stop=True).then_inc(g_pe, 1)
                    pe.wait_ge(g_ac, ACB + AC2 * c + 1)
                    if c >= 1:
                        pe.wait_ge(g_dv, DV2 * (c - 1) + 3)
                    for t in range(4):
                        last = pe.transpose(psT0[:, 128 * t:128 * (t + 1)],
                                            qnat[:, 128 * t:128 * (t + 1)], IDENT[:, :])
                    last.then_inc(g_pe, 1)
                    pe.wait_ge(g_ac, ACB + AC2 * c + 2)
                    for t in range(4):
                        last = pe.transpose(psT1[:, 128 * t:128 * (t + 1)],
                                            knat[:, 128 * t:128 * (t + 1)], IDENT[:, :])
                    last.then_inc(g_pe, 1)
                    for pr in range(2):
                        h0, h1 = 2 * pr, 2 * pr + 1
                        pe.wait_ge(g_dv, DV2 * c + 3 + pr)
                        for j, hh in enumerate((h0, h1)):
                            last = pe.matmul(psS[:, 128 * j:128 * (j + 1)],
                                             kTb[:, 128 * hh:128 * (hh + 1)],
                                             qT[:, 128 * hh:128 * (hh + 1)], start=True, stop=True)
                        last.then_inc(g_pe, 1)
                        pe.wait_ge(g_dv, DV2 * c + 4 + 5 * pr)
                        if pr == 0:
                            if c >= 1:
                                pe.wait_ge(g_ac, ACB + AC2 * (c - 1) + 5)
                        else:
                            pe.wait_ge(g_ac, ACB + AC2 * c + 3)
                        for j, hh in enumerate((h0, h1)):
                            pe.matmul(psL[:, 129 * j:129 * (j + 1)], retb[:, 128 * j:128 * (j + 1)],
                                      vaug[:, 129 * hh:129 * (hh + 1)], start=True, stop=True)
                            pe.matmul(psX[:, 129 * j:129 * (j + 1)],
                                      qT[:, 128 * hh:128 * (hh + 1)],
                                      stateb[hh][:, :], start=True, stop=True)
                            last = pe.matmul(psU[:, 129 * j:129 * (j + 1)],
                                             k2[:, 128 * hh:128 * (hh + 1)],
                                             vaug[:, 129 * hh:129 * (hh + 1)], start=True, stop=True)
                        last.then_inc(g_pe, 1)

            @B.scalar
            def _(ac, hg=hg, g_pe=g_pe, g_ac=g_ac, g_dv=g_dv):
                for c in range(NCH):
                    b = c % 2
                    ac.wait_ge(g_pe, PEB + PE2 * c + 3)
                    ac.activation(qnat[:, :], psA[:, 0:512], AF.Identity, scale=qscb[:, 0:1]).then_inc(g_ac, 1)
                    ac.activation(knat[:, :], psB[:, 0:512], AF.Copy).then_inc(g_ac, 1)
                    for pr in range(2):
                        h0, h1 = 2 * pr, 2 * pr + 1
                        ac.wait_ge(g_pe, PEB + PE2 * c + 7 + 2 * pr)
                        for j, hh in enumerate((h0, h1)):
                            ac.activation(crosse[:, 129 * j:129 * (j + 1)],
                                          psX[:, 129 * j:129 * (j + 1)], AF.Identity,
                                          scale=erB[:, HPG * hg + hh:HPG * hg + hh + 1])
                        ac.drain(); ac.sem_inc(g_ac, 1)
                        ac.wait_ge(g_dv, DV2 * c + 6 + 5 * pr)
                        for hh in (h0, h1):
                            ac.activation(stateb[hh][:, :], state[hh][:, :], AF.Copy)
                        ac.drain(); ac.sem_inc(g_ac, 1)
                    ac.wait_ge(g_dv, DV2 * c + 13)
                    for h in range(HPG):
                        ac.activation(gsq[:, 128 * h:128 * (h + 1)], attnc[:, 128 * h:128 * (h + 1)],
                                      AF.Square, accum_out=gst[:, 4 + h:5 + h])
                    ac.drain(); ac.sem_inc(g_ac, 1)
                    ac.wait_ge(g_dv, DV2 * c + 15)
                    ac.activation(gst[:, 20:24], gst[:, 16:20], AF.Sqrt, bias=epsb[:, 0:1])
                    ac.drain(); ac.sem_inc(g_ac, 1)
                    ac.wait_ge(g_dv, DV2 * c + 16)
                    for h in range(HPG):
                        last = ac.activation(gsq[:, 128 * h:128 * (h + 1)], attnc[:, 128 * h:128 * (h + 1)],
                                             AF.Identity, scale=gst[:, 24 + h:25 + h],
                                             bias=gst[:, 28 + h:29 + h])
                    last.then_inc(g_ac, 1)

            @B.vector
            def _(dv, hg=hg, g_pe=g_pe, g_ac=g_ac, g_dv=g_dv, g_st=g_st, cw=cw):
                for c in range(NCH):
                    b = c % 2
                    dv.wait_ge(g_pe, PEB + PE2 * c + 2)
                    for h in range(HPG):
                        last = dv.tensor_copy(vaug[:, 129 * h:129 * h + 128], psC[:, 128 * h:128 * (h + 1)])
                    last.then_inc(g_dv, 1)
                    dv.wait_ge(g_ac, ACB + AC2 * c + 2)
                    for h in range(HPG):
                        last = dv.tensor_scalar(k2[:, 128 * h:128 * (h + 1)], knat[:, 128 * h:128 * (h + 1)],
                                                ekB[:, HPG * hg + h:HPG * hg + h + 1], None, OP.mult)
                    last.then_inc(g_dv, 1)
                    dv.wait_ge(g_pe, PEB + PE2 * c + 5)
                    dv.tensor_copy(qT[:, :], psT0[:, 0:512])
                    dv.tensor_copy(kTb[:, :], psT1[:, 0:512]).then_inc(g_dv, 1)
                    for pr in range(2):
                        h0, h1 = 2 * pr, 2 * pr + 1
                        dv.wait_ge(g_pe, PEB + PE2 * c + 6 + 2 * pr)
                        dv.tensor_tensor(retb[:, 0:256], psS[:, 0:256],
                                         Dloc[:, 256 * pr:256 * (pr + 1)], OP.mult).then_inc(g_dv, 1)
                        dv.wait_ge(g_ac, ACB + AC2 * c + 3 + 2 * pr)
                        dv.tensor_add(ataug[:, 0:258], crosse[:, 0:258],
                                      psL[:, 0:258]).then_inc(g_dv, 1)
                        dv.wait_ge(g_pe, PEB + PE2 * c + 7 + 2 * pr)
                        for hh in (h0, h1):
                            j = hh - 2 * pr
                            dv.scalar_tensor_tensor(state[hh][:, :], state[hh][:, :],
                                                    edC[:, HPG * hg + hh:HPG * hg + hh + 1],
                                                    psU[:, 129 * j:129 * (j + 1)], OP.mult, OP.add)
                        dv.drain(); dv.sem_inc(g_dv, 1)
                        for j in range(2):
                            dv.tensor_scalar(rsc[:, j:j + 1], ataug[:, 129 * j + 128:129 * j + 129],
                                             -1.0, None, OP.mult)
                        nop2(dv)
                        for j in range(2):
                            dv.tensor_tensor(rsc[:, 2 + j:3 + j], rsc[:, j:j + 1],
                                             ataug[:, 129 * j + 128:129 * j + 129], OP.max)
                        nop2(dv)
                        dv.tensor_scalar(rsc[:, 4:6], rsc[:, 2:4], 1.0, None, OP.max)
                        nop2(dv)
                        dv.reciprocal(rsc[:, 6:8], rsc[:, 4:6])
                        dv.drain(); dv.sem_inc(g_dv, 1)
                        if c >= 1 and pr == 0:
                            dv.wait_ge(g_ac, ACB + AC2 * (c - 1) + 9)
                        nop2(dv)
                        for j, hh in enumerate((h0, h1)):
                            last = dv.tensor_scalar(attnc[:, 128 * hh:128 * (hh + 1)],
                                                    ataug[:, 129 * j:129 * j + 128],
                                                    rsc[:, 6 + j:7 + j], None, OP.mult)
                        last.then_inc(g_dv, 1)
                    for h in range(HPG):
                        dv.reduce_sum(gst[:, h:h + 1], attnc[:, 128 * h:128 * (h + 1)], axis=AX)
                    dv.drain(); dv.sem_inc(g_dv, 1)
                    dv.wait_ge(g_ac, ACB + AC2 * c + 7)
                    nop2(dv)
                    dv.tensor_scalar(gst[:, 8:12], gst[:, 0:4], INV_HD, None, OP.mult)
                    nop2(dv)
                    dv.tensor_tensor(gst[:, 12:16], gst[:, 8:12], gst[:, 8:12], OP.mult)
                    nop2(dv)
                    dv.scalar_tensor_tensor(gst[:, 16:20], gst[:, 4:8], INV_HD, gst[:, 12:16],
                                            OP.mult, OP.subtract)
                    dv.drain(); dv.sem_inc(g_dv, 1)
                    dv.wait_ge(g_ac, ACB + AC2 * c + 8)
                    dv.reciprocal(gst[:, 24:28], gst[:, 20:24])
                    nop2(dv)
                    dv.scalar_tensor_tensor(gst[:, 28:32], gst[:, 8:12], -1.0, gst[:, 24:28],
                                            OP.mult, OP.mult)
                    dv.drain(); dv.sem_inc(g_dv, 1)
                    dv.wait_ge(g_ac, ACB + AC2 * c + 9)
                    if c >= 1:
                        dv.wait_ge(g_st, 16 * c)
                    dv.tensor_tensor(gsq[:, :], gsq[:, :], gnwB[:, cw:cw + WG], OP.mult)
                    dv.tensor_tensor(gsq[:, :], gsq[:, :], gnbB[:, cw:cw + WG], OP.add)
                    dv.tensor_tensor(Pbuf[:, :], gsq[:, :], big[:, RG + 512 * c:RG + 512 * (c + 1)],
                                     OP.mult).then_inc(g_dv, 1)

            @B.gpsimd
            def _(pl, g_dv=g_dv, g_st=g_st, cw=cw):
                for c in range(NCH):
                    pl.wait_ge(g_dv, DV2 * c + DV2)
                    pl.dma_start(dram["Ps"].ap()[128 * c:128 * (c + 1), cw:cw + WG],
                                 Pbuf[:, :]).then_inc(g_st, 16)

            @B.sync
            def _(sp, g_st=g_st):
                sp.wait_ge(g_st, 16 * NCH)

        prev_pl = [(g_wq, 16)]

    if stop_after == "P2":
        dbg_sem = sem("dbgsem")
        with nc.Block() as B:
            @B.gpsimd
            def _(pl):
                pl.wait_ge(prev_pl, prev_pl_cnt)
            @B.sync
            def _(sp):
                sp.sem_inc(dbg_sem, 1)
                sp.wait_ge(dbg_sem, 1)
        stack2.close()
        return nc, stack

    stack2.close()

    stack3 = contextlib.ExitStack()

    def sb3(name, shape, dtype):
        return stack3.enter_context(nc.sbuf_tensor(name, shape, dtype))

    xb = [sb3(f"xc{i}", [128, E], F32) for i in range(2)]
    x1b = [sb3(f"x1c{i}", [128, E], F32) for i in range(2)]
    ub = sb3("uc", [128, E], F32)
    tbuf = sb3("tc", [128, E], BF16)
    boRb = sb3("boRb", [1, E], BF16)
    tbuf_r = tbuf.rearrange("p (k e) -> p k e", k=NK)

    q_wt = sem("q3wt")
    q_p = [sem(f"q3p{i}") for i in range(2)]
    q_x = [sem(f"q3x{i}") for i in range(2)]
    q_pl = sem("q3pl")
    q_pe = sem("q3pe")
    q_ac = sem("q3ac")
    q_dv = sem("q3dv")
    q_st = sem("q3st")
    P3, A3, V3 = 16, 6, 22
    with nc.Block() as B:
        @B.gpsimd
        def _(pl):
            for ps_, pv_ in prev_pl:
                pl.wait_ge(ps_, pv_)
            pl.dma_start(boRb[0:1, :], dram["boR"].ap()[:, :]).then_inc(q_pl, 16)
            src = dram["Wo"].ap().rearrange("(k p) n -> p k n", p=128)
            pl.dma_start(xn2T.rearrange("p (k n) -> p k n", k=NK), src).then_inc(q_pl, 16)
            for c in range(NCH):
                pl.wait_ge(q_dv, V3 * c + V3)
                pl.dma_start(dram["y"].ap()[128 * c:128 * (c + 1), :], xb[c % 2][:, :]).then_inc(q_st, 16)
            pl.wait_ge(q_st, 16 * NCH)

        @B.sync
        def _(sp):
            for ps_, pv_ in prev_pl:
                sp.wait_ge(ps_, pv_)
            sp.dma_start(big[:, :], dram["Wts"].ap()[:, :]).then_inc(q_wt, 16)
            for c in range(NCH):
                b = c % 2
                if c >= 2:
                    sp.wait_ge(q_pe, P3 * (c - 2) + 12)
                    sp.wait_ge(q_st, 16 * (c - 1))
                sp.dma_start(znb[b][:, :], dram["Ps"].ap()[128 * c:128 * (c + 1), :]).then_inc(q_p[b], 16)
                sp.dma_start(xb[b][:, :], dram["x1s"].ap()[128 * c:128 * (c + 1), :]).then_inc(q_x[b], 16)

        @B.tensor
        def _(pe):
            pe.wait_ge(q_pl, 32)
            for c in range(NCH):
                b = c % 2
                pe.wait_ge(q_p[b], 16 * (c // 2 + 1))
                for g in range(4):
                    if g >= 2:
                        pe.wait_ge(q_dv, V3 * c + (g - 2) + 1)
                    elif c > 0:
                        pe.wait_ge(q_dv, V3 * (c - 1) + 15 + (g + 2))
                    for t in range(4):
                        k = 4 * g + t
                        last = pe.transpose(psTx[g % 2][:, 128 * t:128 * (t + 1)],
                                            znb[b][:, 128 * k:128 * (k + 1)], IDENT[:, :])
                    last.then_inc(q_pe, 1)
                pe.wait_ge(q_dv, V3 * c + 4)
                for nb in range(4):
                    bank = psA if nb % 2 == 0 else psB
                    if nb >= 2:
                        pe.wait_ge(q_dv, V3 * c + 4 + (nb - 2) + 1)
                    elif c > 0:
                        pe.wait_ge(q_dv, V3 * (c - 1) + 19 + (nb + 2))
                    for k in range(NK):
                        pe.matmul(bank[:, 0:512], tbuf[:, 128 * k:128 * (k + 1)],
                                  xn2T_r[:, k, 512 * nb:512 * (nb + 1)], start=(k == 0), stop=False)
                    pe.matmul(bank[:, 0:512], onesb[0:1, 0:128], boRb[0:1, 512 * nb:512 * (nb + 1)],
                              start=False, stop=True).then_inc(q_pe, 1)
                pe.wait_ge(q_ac, A3 * c + 6)
                for g in range(4):
                    if g >= 2:
                        pe.wait_ge(q_dv, V3 * c + 15 + (g - 2))
                    else:
                        pe.wait_ge(q_dv, V3 * c + (g + 2) + 1)
                    for t in range(4):
                        k = 4 * g + t
                        last = pe.transpose(psTx[g % 2][:, 128 * t:128 * (t + 1)],
                                            znb[b][:, 128 * k:128 * (k + 1)], IDENT[:, :])
                    last.then_inc(q_pe, 1)
                pe.wait_ge(q_dv, V3 * c + 18)
                if c == 0:
                    pe.wait_ge(q_wt, 16)
                for nb in range(4):
                    bank = psA if nb % 2 == 0 else psB
                    if nb >= 2:
                        pe.wait_ge(q_dv, V3 * c + 18 + (nb - 2) + 1)
                    else:
                        pe.wait_ge(q_dv, V3 * c + 4 + (nb + 2) + 1)
                    for k in range(NK):
                        pe.matmul(bank[:, 0:512], tbuf[:, 128 * k:128 * (k + 1)],
                                  big_r[:, k, 512 * nb:512 * (nb + 1)], start=(k == 0), stop=False)
                    pe.matmul(bank[:, 0:512], onesb[0:1, 0:128],
                              lnbrowb[0:1, 512 * nb:512 * (nb + 1)],
                              start=False, stop=True).then_inc(q_pe, 1)

        @B.scalar
        def _(ac):
            for c in range(NCH):
                b = c % 2
                ac.wait_ge(q_dv, V3 * c + 8)
                ac.activation(ub[:, :], x1b[b][:, :], AF.Square,
                              accum_out=stt[b][:, 4:5])
                ac.drain(); ac.sem_inc(q_ac, 1)
                ac.wait_ge(q_dv, V3 * c + 10)
                ac.activation(stt[b][:, 12:13], stt[b][:, 10:11], AF.Sqrt, bias=epsb[:, 0:1])
                ac.drain(); ac.sem_inc(q_ac, 1)
                ac.wait_ge(q_dv, V3 * c + 11)
                ac.activation(ub[:, :], x1b[b][:, :], AF.Identity, scale=stt[b][:, 13:14],
                              bias=stt[b][:, 14:15]).then_inc(q_ac, 1)
                ac.wait_ge(q_pe, P3 * c + 4)
                ac.activation(znb[b][:, :], ub[:, :], AF.Square,
                              accum_out=stt[b][:, 20:21])
                ac.drain(); ac.sem_inc(q_ac, 1)
                ac.wait_ge(q_dv, V3 * c + 13)
                ac.activation(stt[b][:, 28:29], stt[b][:, 26:27], AF.Sqrt, bias=epsb[:, 0:1])
                ac.drain(); ac.sem_inc(q_ac, 1)
                ac.wait_ge(q_dv, V3 * c + 14)
                ac.activation(znb[b][:, :], ub[:, :], AF.Identity, scale=stt[b][:, 29:30],
                              bias=stt[b][:, 30:31]).then_inc(q_ac, 1)

        @B.vector
        def _(dv):
            for c in range(NCH):
                b = c % 2
                for g in range(4):
                    dv.wait_ge(q_pe, P3 * c + g + 1)
                    dv.tensor_copy(tbuf_r[:, 4 * g:4 * g + 4, :],
                                   psT_r[g % 2][:, 0:4, :]).then_inc(q_dv, 1)
                for nb in range(4):
                    dv.wait_ge(q_pe, P3 * c + 4 + nb + 1)
                    if nb == 0:
                        dv.wait_ge(q_x[b], 16 * (c // 2 + 1))
                    bank = psA if nb % 2 == 0 else psB
                    dv.tensor_add(x1b[b][:, 512 * nb:512 * (nb + 1)],
                                  xb[b][:, 512 * nb:512 * (nb + 1)],
                                  bank[:, 0:512]).then_inc(q_dv, 1)
                dv.reduce_sum(stt[b][:, 0:1], x1b[b][:, :], axis=AX)
                dv.drain(); dv.sem_inc(q_dv, 1)
                dv.wait_ge(q_ac, A3 * c + 1)
                stats3(dv, b, 0); dv.drain(); dv.sem_inc(q_dv, 1)
                dv.wait_ge(q_ac, A3 * c + 2)
                statsfin(dv, b, 0); dv.drain(); dv.sem_inc(q_dv, 1)
                dv.wait_ge(q_ac, A3 * c + 3)
                dv.reduce_sum(stt[b][:, 16:17], ub[:, :], axis=AX)
                dv.drain(); dv.sem_inc(q_dv, 1)
                dv.wait_ge(q_ac, A3 * c + 4)
                stats3(dv, b, 16); dv.drain(); dv.sem_inc(q_dv, 1)
                dv.wait_ge(q_ac, A3 * c + 5)
                statsfin(dv, b, 16); dv.drain(); dv.sem_inc(q_dv, 1)
                for g in range(4):
                    dv.wait_ge(q_pe, P3 * c + 8 + g + 1)
                    dv.tensor_copy(tbuf_r[:, 4 * g:4 * g + 4, :],
                                   psT_r[g % 2][:, 0:4, :]).then_inc(q_dv, 1)
                for nb in range(4):
                    dv.wait_ge(q_pe, P3 * c + 12 + nb + 1)
                    if nb == 0 and c >= 2:
                        dv.wait_ge(q_st, 16 * (c - 1))
                    bank = psA if nb % 2 == 0 else psB
                    dv.tensor_add(xb[b][:, 512 * nb:512 * (nb + 1)],
                                  x1b[b][:, 512 * nb:512 * (nb + 1)],
                                  bank[:, 0:512]).then_inc(q_dv, 1)

    stack3.close()
    return nc, stack


def _host_inputs(inputs, core):
    f = np.ascontiguousarray
    d = {
        "x": f(np.asarray(inputs["L0_attn_x"])[core]),
        "Wf": f(np.asarray(inputs["L0_ffn_W"])),
        "Wq": f(np.asarray(inputs["L0_attn_Wq"])),
        "Wk": f(np.asarray(inputs["L0_attn_Wk"])),
        "Wv": f(np.asarray(inputs["L0_attn_Wv"])),
        "Wg": f(np.asarray(inputs["L0_attn_Wg"])),
        "Wo": f(np.asarray(inputs["L0_attn_Wo"])),
        "lnwT": f(np.asarray(inputs["L0_ffn_ln_w"]).reshape(NK, 128).T),
        "lnbT": f(np.asarray(inputs["L0_ffn_ln_b"]).reshape(NK, 128).T),
        "bqR": f(np.asarray(inputs["L0_attn_bq"]).reshape(1, E)),
        "bkR": f(np.asarray(inputs["L0_attn_bk"]).reshape(1, E)),
        "bvR": f(np.asarray(inputs["L0_attn_bv"]).reshape(1, E)),
        "bgR": f(np.asarray(inputs["L0_attn_bg"]).reshape(1, E)),
        "boR": f(np.asarray(inputs["L0_attn_bo"]).reshape(1, E)),
        "gnwB": f(np.broadcast_to(np.asarray(inputs["L0_attn_gn_w"]), (128, E))),
        "gnbB": f(np.broadcast_to(np.asarray(inputs["L0_attn_gn_b"]), (128, E))),
        "decB": f(np.broadcast_to(np.asarray(inputs["L0_attn_decay"]), (128, H))),
    }
    ii = np.arange(128, dtype=np.float32)
    diff = ii[None, :] - ii[:, None]
    d["DIFFT"] = f(np.where(diff >= 0, diff, BIGNEG).astype(np.float32))
    d["ROWP1"] = f((ii + 1.0).reshape(128, 1))
    d["R127M"] = f((127.0 - ii).reshape(128, 1))
    d["IDENT"] = f(np.eye(128, dtype=ml_dtypes.bfloat16))
    return d


def kernel(**inputs):
    nc, stack = build(debug=False)
    in_maps = [_host_inputs(inputs, core) for core in range(8)]
    res = run_bass_kernel_spmd(nc, in_maps, core_ids=list(range(8)))
    out = np.stack([np.asarray(res.results[i]["y"]) for i in range(8)], axis=0)
    return np.ascontiguousarray(out.astype(np.float32, copy=False))
